# version 03 (final)
# baseline (speedup 1.0000x reference)
import numpy as np
import ml_dtypes
import concourse.bass as bass
import concourse.mybir as mybir
from concourse.bass_utils import run_bass_kernel_spmd
from contextlib import ExitStack, contextmanager

F32 = mybir.dt.float32
BF16 = mybir.dt.bfloat16
I32 = mybir.dt.int32
AF = mybir.ActivationFunctionType
ALU = mybir.AluOpType
AX = mybir.AxisListType

ENGINES = ("tensor", "vector", "scalar", "gpsimd", "sync")


class MK:

    def __init__(self, nc, ctx):
        self.nc = nc
        self.ctx = ctx
        self.ops = {e: [] for e in ENGINES}
        self.glob = []
        self.last_writer = {}
        self.readers = {}
        self.sbuf_bytes = 0
        self._consts = {}
        self._const_tile = None
        self._const_n = 0

    def sbuf(self, name, shape, dtype):
        name = getattr(self, "prefix", "") + name
        t = self.ctx.enter_context(self.nc.sbuf_tensor(name, list(shape), dtype))
        self.sbuf_bytes += int(np.prod(shape[1:])) * mybir.dt.size(dtype)
        return t

    def psum(self, name, shape, dtype=F32):
        name = getattr(self, "prefix", "") + name
        return self.ctx.enter_context(self.nc.psum_tensor(name, list(shape), dtype))

    @contextmanager
    def arena(self, prefix):
        old_ctx, old_pref = self.ctx, getattr(self, "prefix", "")
        s = ExitStack()
        self.ctx, self.prefix = s, prefix
        try:
            with s:
                yield
        finally:
            self.ctx, self.prefix = old_ctx, old_pref

    def const(self, val):
        val = float(val)
        if self._const_tile is None:
            self._const_tile = self.ctx.enter_context(
                self.nc.sbuf_tensor("mk_consts", [128, 64], F32))
        if val not in self._consts:
            k = self._const_n
            assert k < 64, "too many consts"
            self._const_n += 1
            tile = self._const_tile
            self.op("gpsimd", lambda g: g.memset(tile[:, k:k + 1], val),
                    reads=[], writes=[("mkconst", k)])
            self._consts[val] = k
        k = self._consts[val]
        return self._const_tile[:, k:k + 1]

    def cdep(self, val):
        return [("mkconst", self._consts[float(val)])]

    def op(self, eng, fn, reads=(), writes=(), deps=(), dma=False):
        import sys as _s
        f = _s._getframe(1)
        if f.f_code.co_name == "op":
            f = _s._getframe(2)
        _line = f.f_lineno
        tok = (eng, len(self.ops[eng]))
        dd = set()
        for t in deps:
            if t is not None:
                dd.add(t)
        for k in reads:
            t = self.last_writer.get(k)
            if t is not None:
                dd.add(t)
        for k in writes:
            t = self.last_writer.get(k)
            if t is not None:
                dd.add(t)
            for t2 in self.readers.get(k, ()):
                dd.add(t2)
        dd.discard(tok)
        for k in reads:
            self.readers.setdefault(k, []).append(tok)
        for k in writes:
            self.last_writer[k] = tok
            self.readers[k] = []
        self.ops[eng].append(dict(fn=fn, deps=dd, dma=dma, inc=False,
                                  tok=tok, line=_line))
        self.glob.append(tok)
        return tok

    def dma(self, eng, fn, reads=(), writes=(), deps=()):
        return self.op(eng, fn, reads, writes, deps, dma=True)

    def finalize(self):
        nc = self.nc
        csem = {e: self.ctx.enter_context(nc.semaphore(f"c_{e}")) for e in ENGINES}
        dsem = {e: self.ctx.enter_context(nc.semaphore(f"d_{e}")) for e in ENGINES}
        for e in ENGINES:
            for o in self.ops[e]:
                if o["dma"]:
                    o["inc"] = True
        for e in ENGINES:
            for o in self.ops[e]:
                for (pe, pi) in o["deps"]:
                    p = self.ops[pe][pi]
                    if pe != e or p["dma"] or e != "tensor":
                        p["inc"] = True
        cval = {}
        dval = {}
        for e in ENGINES:
            c = 0
            d = 0
            for o in self.ops[e]:
                if o["dma"]:
                    d += 16
                    dval[o["tok"]] = d
                else:
                    if o["inc"]:
                        c += 1
                    cval[o["tok"]] = c
        clocks = {}
        IDX = {}

        def join(a, b):
            for k, v in b.items():
                if a.get(k, -1) < v:
                    a[k] = v

        prev_on = {e: None for e in ENGINES}
        waits = {}
        for tok in self.glob:
            e, i = tok
            o = self.ops[e][i]
            know = {}
            if prev_on[e] is not None:
                know.update(clocks[prev_on[e]])
            w = []
            for (pe, pi) in sorted(o["deps"]):
                p = self.ops[pe][pi]
                kind = "d" if p["dma"] else "c"
                if not p["dma"] and pe == e and e == "tensor":
                    join(know, clocks[(pe, pi)])
                    continue
                if know.get((pe, kind), -1) >= pi:
                    continue
                sem = dsem[pe] if p["dma"] else csem[pe]
                val = dval[(pe, pi)] if p["dma"] else cval[(pe, pi)]
                w.append((sem, val))
                know[(pe, kind)] = pi
                join(know, clocks[(pe, pi)])
            wd = {}
            for s, v in w:
                wd[s] = max(wd.get(s, 0), v)
            waits[tok] = list(wd.items())
            clocks[tok] = dict(know)
            prev_on[e] = tok

        block = self.ctx.enter_context(nc.Block())

        def emit_for(e):
            def body(eng):
                for o in self.ops[e]:
                    for s, v in waits[o["tok"]]:
                        eng.wait_ge(s, v)
                    ins = o["fn"](eng)
                    if o["inc"]:
                        sem = dsem[e] if o["dma"] else csem[e]
                        amt = 16 if o["dma"] else 1
                        ins.then_inc(sem, amt)
            return body

        for e in ENGINES:
            if self.ops[e]:
                getattr(block, e)(emit_for(e))

    def barrier_token(self, keys, eng="gpsimd"):
        return self.op(eng, lambda g: g.engine_nop(), reads=list(keys), writes=[])

    def wait_all_outputs(self, out_keys, eng="gpsimd"):
        self.op(eng, lambda g: g.engine_nop(), reads=list(out_keys))

    def fence(self, *prefixes):
        toks = set()

        def match(k):
            for p in prefixes:
                if k == p or (isinstance(k, tuple) and p in k):
                    return True
            return False

        for k, t in self.last_writer.items():
            if match(k):
                toks.add(t)
        for k, ts in self.readers.items():
            if match(k):
                toks.update(ts)
        return sorted(toks)




CS_ACT = [128.0, 64.0, 32.0, 16.0, 8.0, 4.0, 2.0, 1.0]


def mk_barrier(mk):
    lasts = [mk.ops[e][-1]["tok"] for e in ENGINES if mk.ops[e]]
    bar = mk.op("gpsimd", lambda g: g.engine_nop(), deps=lasts)
    mk._barrier = bar
    return bar


def _patch_barrier(MKcls):
    orig = MKcls.op

    def op(self, eng, fn, reads=(), writes=(), deps=(), dma=False):
        b = getattr(self, "_barrier", None)
        if b is not None and b != ("gpsimd", len(self.ops["gpsimd"]) - 1):
            deps = list(deps) + [b]
        elif b is not None:
            deps = list(deps) + [b]
        return orig(self, eng, fn, reads, writes, deps, dma)

    MKcls.op = op


_patch_barrier(MK)


def emit_sign_floor(mk, K, fin, fkey, r1, k1, r2, k2, rb, kb, xq_out, xqkey, F,
                    upd_engs=("vector",)):
    half = mk.const(0.5)
    cur, curkey = fin, fkey
    pair = [(r1, k1), (r2, k2)]
    for k, c in enumerate(CS_ACT):
        mk.op("scalar", lambda s, cur=cur, F=F, half=half: s.activation(
            rb[:, :F], cur[:, :F], AF.Sign, bias=half),
            reads=[curkey] + mk.cdep(0.5), writes=[kb])
        nxt, nxtkey = pair[k % 2]
        eng = upd_engs[k % len(upd_engs)]
        mk.op(eng, lambda v, nxt=nxt, cur=cur, c=c, F=F: v.scalar_tensor_tensor(
            nxt[:, :F], rb[:, :F], -c / 2.0, cur[:, :F], ALU.mult, ALU.add),
            reads=[kb, curkey], writes=[nxtkey])
        cur, curkey = nxt, nxtkey
    mk.op("vector", lambda v, cur=cur, F=F: v.scalar_tensor_tensor(
        xq_out[:, :F], fin[:, :F], -0.5, cur[:, :F], ALU.add, ALU.subtract),
        reads=[fkey, curkey], writes=[xqkey])


def alloc_ffn0_sb(mk, NT):
    sb = {}
    sb["A1"] = mk.sbuf("A1", [128, 32768], BF16)
    sb["A2"] = mk.sbuf("A2", [128, 12288], F32)
    sb["pb"] = [mk.psum(f"pb{i}", [128, 512], F32) for i in range(8)]
    sb["idt"] = mk.sbuf("idt", [128, 128], F32)
    sb["stats"] = mk.sbuf("stats", [128, NT * 16], F32)
    sb["gumax"] = mk.sbuf("gumax", [128, max(NT, 1)], F32)
    sb["wsc"] = mk.sbuf("wsc", [128, 16], F32)
    sb["sc1"] = mk.sbuf("sc1", [1, 8], F32)
    sb["accw"] = mk.sbuf("accw", [128, 4], F32)
    sb["ones_col"] = mk.sbuf("ones_col", [128, 1], F32)
    sb["ones_row"] = mk.sbuf("ones_row", [1, 128], F32)
    sb["evt"] = mk.sbuf("evt", [128, 512 * 3], F32)
    sb["gus"] = mk.sbuf("gus", [128, 4096], F32)
    sb["xin_t"] = mk.sbuf("xin_t", [128, 4096], F32)
    sb["pnt"] = mk.sbuf("pnt", [128, 4096], F32)
    sb["wdq"] = mk.sbuf("wdq", [128, 8192], BF16)
    sb["rtmp"] = mk.sbuf("rtmp", [128, 66], F32)
    return sb


def emit_ffn0(mk, tag, T, x_in, extras, x_out, wgT, wuT, wdT, gu_hbm, iden_ap, sb):
    P = 128
    NT = T // P
    assert NT % 2 == 0 or NT == 1
    A1, A2 = sb["A1"], sb["A2"]
    pb = sb["pb"]
    idt, stats, gumax, wsc, sc1, accw = sb["idt"], sb["stats"], sb["gumax"], sb["wsc"], sb["sc1"], sb["accw"]
    evt, gus, xin_t, pnt, wdq, rtmp = sb["evt"], sb["gus"], sb["xin_t"], sb["pnt"], sb["wdq"], sb["rtmp"]
    SC = 16
    K = lambda *a: (tag,) + a
    Ar = lambda i: A2[:, 4096 + i * 2048: 4096 + (i + 1) * 2048]
    ArK = lambda i: K("Ar", i)
    grow = A2[:, 0:4096]

    if not sb.get("_init"):
        sb["_init"] = True
        mk.dma("sync", lambda q: q.dma_start(out=idt[:], in_=iden_ap), writes=["iden"])
        mk.op("vector", lambda v: v.memset(sb["ones_col"][:], 1.0), writes=["ones_col"])
        mk.op("vector", lambda v: v.memset(sb["ones_row"][:], 1.0), writes=["ones_row"])

    mk.op("vector", lambda v: v.memset(accw[:, 0:4], 0.0), writes=[K("accw")])
    mk.op("vector", lambda v: v.memset(gumax[:, 0:NT], 0.0), writes=[K("gumax")])
    slots = [A2[:, 0:4096], A2[:, 4096:8192]]
    widx = 0
    for wi, wap in enumerate([wgT, wuT, wdT]):
        for c in range(32):
            s = slots[widx % 2]
            skey = K("ssl", widx % 2)
            if wi < 2:
                src = wap.rearrange("(r p) (h d) -> p r h d", p=128, h=2)[:, c // 2, c % 2, :]
            else:
                src = wap.rearrange("(c a p) d -> p c a d", p=128, a=2)[:, c, :, :]
            mk.dma("sync", lambda q, s=s, src=src: q.dma_start(out=s, in_=src), writes=[skey])
            mk.op("vector", lambda v, s=s: v.reduce_sum(rtmp[:, 0:32], s.rearrange("p (a d) -> p a d", a=32), axis=AX.X, apply_absolute_value=True),
                  reads=[skey], writes=[K("rt1")])
            mk.op("vector", lambda v: v.reduce_sum(rtmp[:, 64:65], rtmp[:, 0:32], axis=AX.X),
                  reads=[K("rt1")], writes=[K("rt2")])
            mk.op("vector", lambda v, wi=wi: v.tensor_add(accw[:, wi:wi + 1], accw[:, wi:wi + 1], rtmp[:, 64:65]),
                  reads=[K("rt2"), K("accw")], writes=[K("accw")])
            widx += 1
    mk.op("tensor", lambda t: t.matmul(pb[0][0:1, 0:3], sb["ones_col"][:, 0:1], accw[:, 0:3], start=True, stop=True),
          reads=[K("accw"), "ones_col"], writes=[K("pbw")])
    mk.op("vector", lambda v: v.tensor_scalar(sc1[0:1, 0:3], pb[0][0:1, 0:3], 1.0 / (1 << 24), 1e-5, ALU.mult, ALU.add),
          reads=[K("pbw")], writes=[K("sc1")])
    mk.op("vector", lambda v: v.tensor_scalar(sc1[0:1, 4:7], sc1[0:1, 0:3], -1.0, None, ALU.mult),
          reads=[K("sc1")], writes=[K("sc1n")])
    mk.op("tensor", lambda t: t.matmul(pb[1][:, 0:3], sb["ones_row"][0:1, :], sc1[0:1, 0:3], start=True, stop=True),
          reads=[K("sc1"), "ones_row"], writes=[K("pbb")])
    mk.op("tensor", lambda t: t.matmul(pb[2][:, 0:3], sb["ones_row"][0:1, :], sc1[0:1, 4:7], start=True, stop=True),
          reads=[K("sc1n"), "ones_row"], writes=[K("pbb2")])
    mk.op("vector", lambda v: v.tensor_copy(wsc[:, 0:3], pb[1][:, 0:3]),
          reads=[K("pbb")], writes=[K("ws")])
    mk.op("vector", lambda v: v.tensor_copy(wsc[:, 3:6], pb[2][:, 0:3]),
          reads=[K("pbb2"), K("ws")], writes=[K("ws")])
    for wi in range(3):
        mk.op("vector", lambda v, wi=wi: v.tensor_scalar(wsc[:, 8 + 2 * wi:9 + 2 * wi], wsc[:, wi:wi + 1], 0.5, None, ALU.mult),
              reads=[K("ws")], writes=[K("h", wi)])
        mk.op("vector", lambda v, wi=wi: v.tensor_scalar(wsc[:, 9 + 2 * wi:10 + 2 * wi], wsc[:, wi:wi + 1], -0.5, None, ALU.mult),
              reads=[K("ws")], writes=[K("nh", wi)])
    mk_barrier(mk)

    for ti in range(NT):
        par = ti % 2
        xt = xin_t[:, par * 2048:(par + 1) * 2048]
        xk = K("xinb", par)
        mk.dma("scalar", lambda q, xt=xt, ti=ti: q.dma_start(out=xt, in_=x_in[ti * P:(ti + 1) * P, :]), writes=[xk])
        for ei, eap in enumerate(extras):
            et = gus[:, ei * 2048:(ei + 1) * 2048]
            ek = K("extb", ei)
            mk.dma("scalar", lambda q, et=et, eap=eap, ti=ti: q.dma_start(out=et, in_=eap[ti * P:(ti + 1) * P, :]), writes=[ek])
            mk.op("vector", lambda v, xt=xt, et=et: v.tensor_add(xt, xt, et),
                  reads=[xk, ek], writes=[xk])
        st = lambda c, ti=ti: stats[:, ti * SC + c: ti * SC + c + 1]
        stk = lambda c, ti=ti: K("st", ti, c)
        mk.op("vector", lambda v, xt=xt, st=st: v.reduce_sum(st(0), xt, axis=AX.X), reads=[xk], writes=[stk(0)])
        mk.op("vector", lambda v, st=st: v.tensor_scalar(st(1), st(0), -1.0 / 2048.0, None, ALU.mult),
              reads=[stk(0)], writes=[stk(1)])
        mk.op("vector", lambda v, xt=xt, st=st: v.reduce_max(st(5), xt, axis=AX.X), reads=[xk], writes=[stk(5)])
        mk.op("vector", lambda v, xt=xt, st=st: v.tensor_reduce(st(6), xt, axis=AX.X, op=ALU.min), reads=[xk], writes=[stk(6)])
        xc = Ar(0)
        mk.op("vector", lambda v, xt=xt, st=st, xc=xc: v.tensor_scalar(xc, xt, st(1), None, ALU.add),
              reads=[xk, stk(1)], writes=[ArK(0)])
        mk.op("scalar", lambda s, xc=xc, st=st: s.activation(Ar(3), xc, AF.Square, accum_out=st(7)),
              reads=[ArK(0)], writes=[ArK(3), stk(7)])
        mk.op("vector", lambda v, st=st: v.tensor_scalar(st(7), st(7), 1.0 / 2048.0, 1e-5, ALU.mult, ALU.add),
              reads=[stk(7)], writes=[stk(7)])
        mk.op("scalar", lambda s, st=st: s.sqrt(st(7), st(7)), reads=[stk(7)], writes=[stk(7)])
        mk.op("vector", lambda v, st=st: v.reciprocal(st(2), st(7)), reads=[stk(7)], writes=[stk(2)])
        mk.op("vector", lambda v, st=st: v.tensor_add(st(5), st(5), st(1)), reads=[stk(5), stk(1)], writes=[stk(5)])
        mk.op("vector", lambda v, st=st: v.scalar_tensor_tensor(st(6), st(6), -1.0, st(1), ALU.mult, ALU.subtract),
              reads=[stk(6), stk(1)], writes=[stk(6)])
        mk.op("vector", lambda v, st=st: v.tensor_max(st(5), st(5), st(6)), reads=[stk(5), stk(6)], writes=[stk(5)])
        mk.op("vector", lambda v, st=st: v.tensor_mul(st(5), st(5), st(2)), reads=[stk(5), stk(2)], writes=[stk(5)])
        mk.op("vector", lambda v, st=st: v.tensor_scalar(st(3), st(5), 1.0 / 127.0, 1e-5, ALU.mult, ALU.add),
              reads=[stk(5)], writes=[stk(3)])
        mk.op("vector", lambda v, st=st: v.reciprocal(st(5), st(3)), reads=[stk(3)], writes=[stk(5)])
        mk.op("vector", lambda v, st=st: v.tensor_mul(st(4), st(5), st(2)), reads=[stk(5), stk(2)], writes=[stk(4)])
        mk.op("vector", lambda v, xc=xc, st=st: v.tensor_scalar(xc, xc, st(4), None, ALU.mult),
              reads=[ArK(0), stk(4)], writes=[ArK(0)])
        emit_sign_floor(mk, K, xc, ArK(0), Ar(1), ArK(1), Ar(2), ArK(2), Ar(3), ArK(3),
                        Ar(1), ArK(1), 2048)
        for kc in range(16):
            b = kc % 2
            mk.op("tensor", lambda t, kc=kc, b=b: t.transpose(pb[b][:, 0:128], Ar(1)[:, kc * 128:(kc + 1) * 128], idt[:]),
                  reads=[ArK(1), "iden"], writes=[K("pb", b)])
            col = (kc * NT + ti) * 128
            mk.op("vector", lambda v, col=col, b=b: v.tensor_copy(A1[:, col:col + 128], pb[b][:, 0:128]),
                  reads=[K("pb", b)], writes=[K("xqT", kc, ti)])
    mk_barrier(mk)

    NB_O = 16
    for ob in range(NB_O):
        for wi, wap in enumerate([wgT, wuT]):
            for h in range(2):
                raw = slots[wi]
                rk = K("braw", wi)
                mk.dma("sync", lambda q, raw=raw, wap=wap, ob=ob, h=h: q.dma_start(
                    out=raw.rearrange("p (kc n) -> p kc n", kc=8),
                    in_=wap.rearrange("(kc p) d -> p kc d", p=128)[:, h * 8:(h + 1) * 8, ob * 512:(ob + 1) * 512]),
                    writes=[rk])
                mk.op("gpsimd", lambda g, raw=raw, wi=wi: g.tensor_scalar(
                    pnt[:], raw, wsc[:, 8 + 2 * wi:9 + 2 * wi], None, ALU.is_gt),
                    reads=[rk, K("h", wi)], writes=[K("pt")])
                wq_s = A1[:, 16384 + wi * 8192 + h * 4096: 16384 + wi * 8192 + (h + 1) * 4096]
                mk.op("vector", lambda v, raw=raw, wq_s=wq_s, wi=wi: v.scalar_tensor_tensor(
                    wq_s, raw, wsc[:, 9 + 2 * wi:10 + 2 * wi], pnt[:], ALU.is_lt, ALU.subtract),
                    reads=[rk, K("nh", wi), K("pt")], writes=[K("bwq", wi, h)])
        for half in range(NT // 4) or [0]:
            tis = list(range(half * 4, min(half * 4 + 4, NT)))
            for kc in range(16):
                for wi in range(2):
                    wq_s = A1[:, 16384 + wi * 8192 + (kc // 8) * 4096: 16384 + wi * 8192 + (kc // 8 + 1) * 4096]
                    wqk = K("bwq", wi, kc // 8)
                    for j, ti in enumerate(tis):
                        bank = j * 2 + wi
                        col = (kc * NT + ti) * 128
                        mk.op("tensor", lambda t, bank=bank, col=col, wq_s=wq_s, kc=kc:
                              t.matmul(pb[bank][:, 0:512], A1[:, col:col + 128], wq_s[:, (kc % 8) * 512:(kc % 8 + 1) * 512],
                                       start=(kc == 0), stop=(kc == 15)),
                              reads=[K("xqT", kc, ti), wqk], writes=[K("pb", bank)])
            for j, ti in enumerate(tis):
                st = lambda c, ti=ti: stats[:, ti * SC + c: ti * SC + c + 1]
                stk = lambda c, ti=ti: K("st", ti, c)
                mk.op("vector", lambda v, st=st: v.tensor_mul(st(8), st(3), wsc[:, 3:4]),
                      reads=[stk(3), K("ws")], writes=[stk(8)])
                mk.op("vector", lambda v, st=st: v.tensor_mul(st(9), st(3), wsc[:, 4:5]),
                      reads=[stk(3), K("ws")], writes=[stk(9)])
                bg, bu = j * 2, j * 2 + 1
                ev_g = evt[:, 0:512]
                guv = gus[:, j * 512: (j + 1) * 512]
                mk.op("scalar", lambda s, bg=bg, st=st, ev_g=ev_g: s.activation(ev_g, pb[bg][:, 0:512], AF.Sigmoid, scale=st(8)),
                      reads=[K("pb", bg), stk(8)], writes=[K("evg")])
                mk.op("vector", lambda v, bg=bg, st=st, ev_g=ev_g: v.scalar_tensor_tensor(ev_g, pb[bg][:, 0:512], st(8), ev_g, ALU.mult, ALU.mult),
                      reads=[K("pb", bg), stk(8), K("evg")], writes=[K("evg")])
                mk.op("vector", lambda v, bu=bu, st=st, guv=guv: v.tensor_scalar(guv, pb[bu][:, 0:512], st(9), None, ALU.mult),
                      reads=[K("pb", bu), stk(9)], writes=[K("guv", j)])
                mk.op("vector", lambda v, guv=guv, ev_g=ev_g: v.tensor_mul(guv, guv, ev_g),
                      reads=[K("guv", j), K("evg")], writes=[K("guv", j)])
                mk.op("vector", lambda v, guv=guv: v.reduce_max(rtmp[:, 65:66], guv, axis=AX.X, apply_absolute_value=True),
                      reads=[K("guv", j)], writes=[K("rt3")])
                mk.op("vector", lambda v, ti=ti: v.tensor_max(gumax[:, ti:ti + 1], gumax[:, ti:ti + 1], rtmp[:, 65:66]),
                      reads=[K("rt3"), K("gumax")], writes=[K("gumax")])
                mk.dma("scalar", lambda q, guv=guv, ti=ti, ob=ob: q.dma_start(
                    out=gu_hbm[ti * P:(ti + 1) * P, ob * 512:(ob + 1) * 512], in_=guv),
                    reads=[K("guv", j)], writes=[K("guh", ti, ob)])
    mk_barrier(mk)

    NH = max(NT // 4, 1)
    for th in range(NH):
        tis = list(range(th * 4, min(th * 4 + 4, NT)))
        for tloc, ti in enumerate(tis):
            st = lambda c, ti=ti: stats[:, ti * SC + c: ti * SC + c + 1]
            stk = lambda c, ti=ti: K("st", ti, c)
            mk.op("vector", lambda v, ti=ti, st=st: v.tensor_scalar(st(10), gumax[:, ti:ti + 1], 1.0 / 127.0, 1e-5, ALU.mult, ALU.add),
                  reads=[K("gumax")], writes=[stk(10)])
            mk.op("vector", lambda v, st=st: v.reciprocal(st(11), st(10)), reads=[stk(10)], writes=[stk(11)])
            for hq in range(2):
                gk = K("grow")
                mk.dma("scalar", lambda q, ti=ti, hq=hq: q.dma_start(out=grow, in_=gu_hbm[ti * P:(ti + 1) * P, hq * 4096:(hq + 1) * 4096]),
                       reads=[K("guh", ti, o) for o in range(hq * 8, hq * 8 + 8)], writes=[gk])
                for qq in range(2):
                    src = grow[:, qq * 2048:(qq + 1) * 2048]
                    f0 = Ar(0)
                    mk.op("vector", lambda v, src=src, st=st, f0=f0: v.tensor_scalar(f0, src, st(11), None, ALU.mult),
                          reads=[gk, stk(11)], writes=[ArK(0)])
                    emit_sign_floor(mk, K, f0, ArK(0), Ar(1), ArK(1), Ar(2), ArK(2), Ar(3), ArK(3),
                                    Ar(1), ArK(1), 2048)
                    for kk in range(16):
                        kc2 = (hq * 2 + qq) * 16 + kk
                        b = kk % 2
                        mk.op("tensor", lambda t, kk=kk, b=b: t.transpose(pb[b][:, 0:128], Ar(1)[:, kk * 128:(kk + 1) * 128], idt[:]),
                              reads=[ArK(1), "iden"], writes=[K("pb", b)])
                        col = (kc2 * 4 + tloc) * 128
                        mk.op("vector", lambda v, col=col, b=b: v.tensor_copy(A1[:, col:col + 128], pb[b][:, 0:128]),
                              reads=[K("pb", b)], writes=[K("gqT", kc2, tloc)])
        mk_barrier(mk)
        for o2 in range(4):
            for slab in range(8):
                par = slab % 2
                raw = slots[par]
                rk = K("draw", par)
                mk.dma("sync", lambda q, raw=raw, slab=slab, o2=o2: q.dma_start(
                    out=raw.rearrange("p (kc n) -> p kc n", kc=8),
                    in_=wdT.rearrange("(kc p) d -> p kc d", p=128)[:, slab * 8:(slab + 1) * 8, o2 * 512:(o2 + 1) * 512]),
                    writes=[rk])
                mk.op("gpsimd", lambda g, raw=raw: g.tensor_scalar(pnt[:], raw, wsc[:, 12:13], None, ALU.is_gt),
                      reads=[rk, K("h", 2)], writes=[K("pt")])
                wq_s = wdq[:, par * 4096:(par + 1) * 4096]
                wqk = K("dwq", par)
                mk.op("vector", lambda v, raw=raw, wq_s=wq_s: v.scalar_tensor_tensor(
                    wq_s, raw, wsc[:, 13:14], pnt[:], ALU.is_lt, ALU.subtract),
                    reads=[rk, K("nh", 2), K("pt")], writes=[wqk])
                for kci in range(8):
                    kc2 = slab * 8 + kci
                    for tloc, ti in enumerate(tis):
                        col = (kc2 * 4 + tloc) * 128
                        mk.op("tensor", lambda t, tloc=tloc, col=col, wq_s=wq_s, kci=kci, kc2=kc2:
                              t.matmul(pb[tloc][:, 0:512], A1[:, col:col + 128], wq_s[:, kci * 512:(kci + 1) * 512],
                                       start=(kc2 == 0), stop=(kc2 == 63)),
                              reads=[K("gqT", kc2, tloc), wqk], writes=[K("pb", tloc)])
            for tloc, ti in enumerate(tis):
                st = lambda c, ti=ti: stats[:, ti * SC + c: ti * SC + c + 1]
                stk = lambda c, ti=ti: K("st", ti, c)
                mk.op("vector", lambda v, st=st: v.tensor_mul(st(12), st(10), wsc[:, 5:6]),
                      reads=[stk(10), K("ws")], writes=[stk(12)])
                mk.op("vector", lambda v, st=st: v.tensor_scalar(st(12), st(12), 0.5, None, ALU.mult),
                      reads=[stk(12)], writes=[stk(12)])
                ev = evt[:, 512:1024]
                mk.op("vector", lambda v, tloc=tloc, st=st, ev=ev: v.tensor_scalar(ev, pb[tloc][:, 0:512], st(12), None, ALU.mult),
                      reads=[K("pb", tloc), stk(12)], writes=[K("dev")])
                xb2 = evt[:, 1024:1536]
                xk2 = K("dx")
                mk.dma("scalar", lambda q, ti=ti, o2=o2, xb2=xb2: q.dma_start(out=xb2, in_=x_in[ti * P:(ti + 1) * P, o2 * 512:(o2 + 1) * 512]),
                       writes=[xk2])
                for eap in extras:
                    eb = evt[:, 0:512]
                    mk.dma("scalar", lambda q, ti=ti, o2=o2, eb=eb, eap=eap: q.dma_start(out=eb, in_=eap[ti * P:(ti + 1) * P, o2 * 512:(o2 + 1) * 512]),
                           writes=[K("evg")])
                    mk.op("vector", lambda v, xb2=xb2, eb=eb: v.tensor_add(xb2, xb2, eb),
                          reads=[xk2, K("evg")], writes=[xk2])
                mk.op("vector", lambda v, ev=ev, xb2=xb2: v.tensor_add(ev, ev, xb2),
                      reads=[K("dev"), xk2], writes=[K("dev")])
                mk.dma("gpsimd", lambda q, ti=ti, o2=o2, ev=ev: q.dma_start(
                    out=x_out[ti * P:(ti + 1) * P, o2 * 512:(o2 + 1) * 512], in_=ev),
                    reads=[K("dev")], writes=[K("xo", ti, o2)])
        mk_barrier(mk)
    return [K("xo", ti, o2) for ti in range(NT) for o2 in range(4)]


def alloc_att1_sb(mk, NT):
    sb = {}
    sb["A1"] = mk.sbuf("A1", [128, 49152], BF16)
    sb["A2"] = mk.sbuf("A2", [128, 10240], F32)
    sb["pb"] = [mk.psum(f"pb{i}", [128, 512], F32) for i in range(8)]
    sb["idt"] = mk.sbuf("idt", [128, 128], F32)
    sb["cs"] = mk.sbuf("cs", [128, 2 * NT * 64], F32)
    sb["kT"] = mk.sbuf("kT", [128, NT * 128], BF16)
    sb["vt"] = mk.sbuf("vt", [128, NT * 128], BF16)
    sb["wab"] = mk.sbuf("wab", [128, 2 * 8 * 512], BF16)
    sb["wob"] = mk.sbuf("wob", [128, 2 * 4 * 512], BF16)
    sb["stats"] = mk.sbuf("stats", [128, NT * 16], F32)
    sb["sc0"] = mk.sbuf("sc0", [1, 80], F32)
    sb["zc"] = mk.sbuf("zc", [128, 8], F32)
    sb["ones_row"] = mk.sbuf("ones_row", [1, 128], F32)
    sb["evo"] = mk.sbuf("evo", [128, 1024], F32)
    return sb


def emit_att1(mk, tag, NT, x_in, waT, woT, pc_out, iden_ap, sb, hbase=0):
    P = 128
    T = NT * P
    NKC = 16
    H8 = 8
    A1, A2, pb = sb["A1"], sb["A2"], sb["pb"]
    idt, cs, kT, vt = sb["idt"], sb["cs"], sb["kT"], sb["vt"]
    stats, sc0, zc = sb["stats"], sb["sc0"], sb["zc"]
    SC = 16
    K = lambda *a: (tag,) + a
    SM = 1.0 / np.sqrt(128.0)
    R_X = A2[:, 0:2048]
    R_XLN = A2[:, 2048:4096]
    R_QF = A2[:, 4096:5120]
    R_QR = A2[:, 5120:6144]
    R_KV = A2[:, 6144:6400]
    R_KR = A2[:, 6400:6528]
    R_T1 = A2[:, 6528:7040]
    R_T2 = A2[:, 7040:7552]
    R_AT = A2[:, 8192:8192 + NT * 128]
    if not sb.get("_init"):
        sb["_init"] = True
        mk.dma("sync", lambda q: q.dma_start(out=idt[:], in_=iden_ap), writes=["iden"])
        mk.op("vector", lambda v: v.memset(sb["ones_row"][:], 1.0), writes=["ones_row"])

    mk.op("gpsimd", lambda g: g.iota(sc0[0:1, 0:64], pattern=[[1, 64]], base=0,
                                     channel_multiplier=0, allow_small_or_imprecise_dtypes=True),
          writes=[K("jrow")])
    mk.op("vector", lambda v: v.tensor_scalar(sc0[0:1, 64:128] if False else sc0[0:1, 0:64], sc0[0:1, 0:64], -1.0 / 32.0, -1.0 / 128.0, ALU.mult, ALU.add),
          reads=[K("jrow")], writes=[K("jrow")])
    mk.op("vector", lambda v: v.memset(sc0[0:1, 64:65], 10000.0), writes=[K("c10k")])
    mk.op("scalar", lambda s: s.activation(sc0[0:1, 64:65], sc0[0:1, 64:65], AF.Ln),
          reads=[K("c10k")], writes=[K("c10k")])
    mk.op("vector", lambda v: v.tensor_scalar(sc0[0:1, 0:64], sc0[0:1, 0:64], sc0[0:1, 64:65], None, ALU.mult),
          reads=[K("jrow"), K("c10k")], writes=[K("jrow")])
    mk.op("scalar", lambda s: s.activation(sc0[0:1, 0:64], sc0[0:1, 0:64], AF.Exp),
          reads=[K("jrow")], writes=[K("jrow")])
    mk.op("tensor", lambda t: t.matmul(pb[0][:, 0:64], sb["ones_row"][0:1, :], sc0[0:1, 0:64], start=True, stop=True),
          reads=[K("jrow"), "ones_row"], writes=[K("pbp")])
    posb = A2[:, 7552:7616]
    mk.op("vector", lambda v: v.tensor_copy(posb, pb[0][:, 0:64]), reads=[K("pbp")], writes=[K("posb")])
    tv = A2[:, 7616:7616 + NT]
    mk.op("gpsimd", lambda g: g.iota(tv, pattern=[[128, NT]], base=hbase, channel_multiplier=1,
                                     allow_small_or_imprecise_dtypes=True), writes=[K("tv")])
    TP = float(2 * np.pi)
    c1 = float(np.float32(np.ldexp(np.floor(np.ldexp(TP, 10)), -10)))
    c2 = float(np.float32(np.ldexp(np.floor(np.ldexp(TP - c1, 22)), -22)))
    c3 = float(np.float32(TP - c1 - c2))
    h1 = float(np.float32(np.pi / 2))
    h2 = float(np.float32(np.pi / 2 - h1))
    ang = A2[:, 0:NT * 64]
    kf = A2[:, 2048:2048 + NT * 64]
    ki = sb.get("ki")
    if ki is None:
        ki = sb["ki"] = mk.sbuf("att_ki", [128, NT * 64], I32)
    for half in range(2):
        for ti in range(NT):
            mk.op("vector", lambda v, ti=ti: v.tensor_scalar(ang[:, ti * 64:(ti + 1) * 64], posb, tv[:, ti:ti + 1], None, ALU.mult),
                  reads=[K("posb"), K("tv")], writes=[K("ang")])
        if half == 1:
            mk.op("vector", lambda v: v.tensor_scalar(ang[:], ang[:], h1, None, ALU.add),
                  reads=[K("ang")], writes=[K("ang")])
            mk.op("vector", lambda v: v.tensor_scalar(ang[:], ang[:], h2, None, ALU.add),
                  reads=[K("ang")], writes=[K("ang")])
        mk.op("vector", lambda v: v.tensor_scalar(kf[:], ang[:], 1.0 / TP, None, ALU.mult),
              reads=[K("ang")], writes=[K("kf")])
        mk.op("vector", lambda v: v.tensor_copy(ki[:], kf[:]), reads=[K("kf")], writes=[K("ki")])
        mk.op("vector", lambda v: v.tensor_copy(kf[:], ki[:]), reads=[K("ki")], writes=[K("kf")])
        dst = cs[:, NT * 64:2 * NT * 64] if half == 0 else cs[:, 0:NT * 64]
        for cc in [c1, c2, c3]:
            mk.op("vector", lambda v, cc=cc: v.scalar_tensor_tensor(ang[:], kf[:], -cc, ang[:], ALU.mult, ALU.add),
                  reads=[K("kf"), K("ang")], writes=[K("ang")])
        mk.op("scalar", lambda s, dst=dst: s.activation(dst, ang[:], AF.Sin),
              reads=[K("ang")], writes=[K("cs", half)])
    mk_barrier(mk)

    for ti in range(NT):
        st = lambda c, ti=ti: stats[:, ti * SC + c: ti * SC + c + 1]
        stk = lambda c, ti=ti: K("st", ti, c)
        xk = K("xt")
        mk.dma("sync", lambda q, ti=ti: q.dma_start(out=R_X, in_=x_in[ti * P:(ti + 1) * P, :]), writes=[xk])
        mk.op("vector", lambda v, st=st: v.reduce_sum(st(0), R_X, axis=AX.X), reads=[xk], writes=[stk(0)])
        mk.op("vector", lambda v, st=st: v.tensor_scalar(st(1), st(0), -1.0 / 2048.0, None, ALU.mult),
              reads=[stk(0)], writes=[stk(1)])
        mk.op("vector", lambda v, st=st: v.tensor_scalar(R_XLN, R_X, st(1), None, ALU.add),
              reads=[xk, stk(1)], writes=[K("xln")])
        mk.op("scalar", lambda s, st=st: s.activation(R_X, R_XLN, AF.Square, accum_out=st(7)),
              reads=[K("xln")], writes=[K("xt"), stk(7)])
        mk.op("vector", lambda v, st=st: v.tensor_scalar(st(7), st(7), 1.0 / 2048.0, 1e-5, ALU.mult, ALU.add),
              reads=[stk(7)], writes=[stk(7)])
        mk.op("scalar", lambda s, st=st: s.sqrt(st(7), st(7)), reads=[stk(7)], writes=[stk(7)])
        mk.op("vector", lambda v, st=st: v.reciprocal(st(2), st(7)), reads=[stk(7)], writes=[stk(2)])
        mk.op("vector", lambda v, st=st: v.tensor_scalar(R_XLN, R_XLN, st(2), None, ALU.mult),
              reads=[K("xln"), stk(2)], writes=[K("xln")])
        for kc in range(NKC):
            b = 6 + kc % 2
            mk.op("tensor", lambda t, kc=kc, b=b: t.transpose(pb[b][:, 0:128], R_XLN[:, kc * 128:(kc + 1) * 128], idt[:]),
                  reads=[K("xln"), "iden"], writes=[K("pb", b)])
            col = (kc * NT + ti) * 128
            mk.op("vector", lambda v, col=col, b=b: v.tensor_copy(A1[:, col:col + 128], pb[b][:, 0:128]),
                  reads=[K("pb", b)], writes=[K("xlnT", kc, ti)])
    mk_barrier(mk)

    OBS = [(0, 512), (512, 512), (1024, 256)]
    for obi, (ocol, owid) in enumerate(OBS):
        for hk in range(2):
            par = (obi * 2 + hk) % 2
            wslot = sb["wab"][:, par * 4096: par * 4096 + 8 * 512]
            wk = K("wab", par)
            mk.dma("gpsimd", lambda q, ocol=ocol, owid=owid, wslot=wslot, hk=hk: q.dma_start(
                out=wslot.rearrange("p (kc n) -> p kc n", kc=8)[:, :, 0:owid],
                in_=waT.rearrange("(kc p) d -> p kc d", p=128)[:, hk * 8:(hk + 1) * 8, ocol:ocol + owid]),
                writes=[wk])
        for tig in range(0, NT, 6):
            tis = list(range(tig, min(tig + 6, NT)))
            for kc in range(NKC):
                par = (obi * 2 + kc // 8) % 2
                wslot = sb["wab"][:, par * 4096: par * 4096 + 8 * 512]
                wk = K("wab", par)
                for j, ti in enumerate(tis):
                    col = (kc * NT + ti) * 128
                    mk.op("tensor", lambda t, j=j, col=col, wslot=wslot, kc=kc, owid=owid:
                          t.matmul(pb[j][:, 0:owid], A1[:, col:col + 128], wslot[:, (kc % 8) * 512:(kc % 8) * 512 + owid],
                                   start=(kc == 0), stop=(kc == NKC - 1)),
                          reads=[K("xlnT", kc, ti), wk], writes=[K("pb", j)])
            for j, ti in enumerate(tis):
                if obi < 2:
                    mk.op("vector", lambda v, j=j, obi=obi: v.tensor_copy(R_QF[:, obi * 512:(obi + 1) * 512], pb[j][:, 0:512]),
                          reads=[K("pb", j)], writes=[K("qf", obi)])
                    csc_ap = cs[:, 0:NT * 64].rearrange("p (t d) -> p t d", t=NT)[:, ti, :].rearrange("p (o d) -> p o d", o=1).broadcast_to([128, 4, 64])
                    ssc_ap = cs[:, NT * 64:].rearrange("p (t d) -> p t d", t=NT)[:, ti, :].rearrange("p (o d) -> p o d", o=1).broadcast_to([128, 4, 64])
                    csc = lambda csc_ap=csc_ap: csc_ap
                    ssc = lambda ssc_ap=ssc_ap: ssc_ap
                    q1 = R_QF[:, obi * 512:(obi + 1) * 512].rearrange("p (h two d) -> p h two d", h=4, two=2)[:, :, 0, :]
                    q2 = R_QF[:, obi * 512:(obi + 1) * 512].rearrange("p (h two d) -> p h two d", h=4, two=2)[:, :, 1, :]
                    o1 = R_QR[:, obi * 512:(obi + 1) * 512].rearrange("p (h two d) -> p h two d", h=4, two=2)[:, :, 0, :]
                    o2 = R_QR[:, obi * 512:(obi + 1) * 512].rearrange("p (h two d) -> p h two d", h=4, two=2)[:, :, 1, :]
                    t1 = R_T1[:, 0:256].rearrange("p (h d) -> p h d", h=4)
                    t2 = R_T2[:, 0:256].rearrange("p (h d) -> p h d", h=4)
                    mk.op("vector", lambda v, q1=q1, csc=csc, t1=t1: v.tensor_tensor(t1, q1, csc(), ALU.mult),
                          reads=[K("qf", obi), K("cs", 1)], writes=[K("t1")])
                    mk.op("vector", lambda v, q2=q2, ssc=ssc, t2=t2: v.tensor_tensor(t2, q2, ssc(), ALU.mult),
                          reads=[K("qf", obi), K("cs", 0)], writes=[K("t2")])
                    mk.op("vector", lambda v, o1=o1, t1=t1, t2=t2: v.tensor_sub(o1, t1, t2),
                          reads=[K("t1"), K("t2")], writes=[K("qr", obi, 0)])
                    mk.op("vector", lambda v, q2=q2, csc=csc, t1=t1: v.tensor_tensor(t1, q2, csc(), ALU.mult),
                          reads=[K("qf", obi), K("cs", 1)], writes=[K("t1")])
                    mk.op("vector", lambda v, q1=q1, ssc=ssc, t2=t2: v.tensor_tensor(t2, q1, ssc(), ALU.mult),
                          reads=[K("qf", obi), K("cs", 0)], writes=[K("t2")])
                    mk.op("vector", lambda v, o2=o2, t1=t1, t2=t2: v.tensor_add(o2, t1, t2),
                          reads=[K("t1"), K("t2")], writes=[K("qr", obi, 1)])
                    for hh in range(4):
                        h = obi * 4 + hh
                        b = 6 + hh % 2
                        mk.op("tensor", lambda t, hh=hh, obi=obi, b=b: t.transpose(
                            pb[b][:, 0:128], R_QR[:, obi * 512 + hh * 128: obi * 512 + (hh + 1) * 128], idt[:]),
                            reads=[K("qr", obi, 0), K("qr", obi, 1), "iden"], writes=[K("pb", b)])
                        col = 32768 + (h * NT + ti) * 128
                        mk.op("vector", lambda v, col=col, b=b: v.tensor_copy(A1[:, col:col + 128], pb[b][:, 0:128]),
                              reads=[K("pb", b)], writes=[K("qT", h, ti)])
                else:
                    mk.op("vector", lambda v, j=j: v.tensor_copy(R_KV, pb[j][:, 0:256]),
                          reads=[K("pb", j)], writes=[K("kv")])
                    kk1 = R_KV[:, 0:128].rearrange("p (h two d) -> p h two d", h=1, two=2)[:, :, 0, :]
                    kk2 = R_KV[:, 0:128].rearrange("p (h two d) -> p h two d", h=1, two=2)[:, :, 1, :]
                    ko1 = R_KR.rearrange("p (h two d) -> p h two d", h=1, two=2)[:, :, 0, :]
                    ko2 = R_KR.rearrange("p (h two d) -> p h two d", h=1, two=2)[:, :, 1, :]
                    csc1_ap = cs[:, 0:NT * 64].rearrange("p (t d) -> p t d", t=NT)[:, ti, :].rearrange("p (o d) -> p o d", o=1)
                    ssc1_ap = cs[:, NT * 64:].rearrange("p (t d) -> p t d", t=NT)[:, ti, :].rearrange("p (o d) -> p o d", o=1)
                    csc1 = lambda csc1_ap=csc1_ap: csc1_ap
                    ssc1 = lambda ssc1_ap=ssc1_ap: ssc1_ap
                    t1h = R_T1[:, 0:64].rearrange("p (h d) -> p h d", h=1)
                    t2h = R_T2[:, 0:64].rearrange("p (h d) -> p h d", h=1)
                    mk.op("vector", lambda v, kk1=kk1, csc1=csc1, t1h=t1h: v.tensor_tensor(t1h, kk1, csc1(), ALU.mult),
                          reads=[K("kv"), K("cs", 1)], writes=[K("t1")])
                    mk.op("vector", lambda v, kk2=kk2, ssc1=ssc1, t2h=t2h: v.tensor_tensor(t2h, kk2, ssc1(), ALU.mult),
                          reads=[K("kv"), K("cs", 0)], writes=[K("t2")])
                    mk.op("vector", lambda v, ko1=ko1, t1h=t1h, t2h=t2h: v.tensor_sub(ko1, t1h, t2h),
                          reads=[K("t1"), K("t2")], writes=[K("kr")])
                    mk.op("vector", lambda v, kk2=kk2, csc1=csc1, t1h=t1h: v.tensor_tensor(t1h, kk2, csc1(), ALU.mult),
                          reads=[K("kv"), K("cs", 1)], writes=[K("t1")])
                    mk.op("vector", lambda v, kk1=kk1, ssc1=ssc1, t2h=t2h: v.tensor_tensor(t2h, kk1, ssc1(), ALU.mult),
                          reads=[K("kv"), K("cs", 0)], writes=[K("t2")])
                    mk.op("vector", lambda v, ko2=ko2, t1h=t1h, t2h=t2h: v.tensor_add(ko2, t1h, t2h),
                          reads=[K("t1"), K("t2")], writes=[K("kr")])
                    b = 6
                    mk.op("tensor", lambda t, b=b: t.transpose(pb[b][:, 0:128], R_KR, idt[:]),
                          reads=[K("kr"), "iden"], writes=[K("pb", b)])
                    mk.op("vector", lambda v, ti=ti, b=b: v.tensor_copy(kT[:, ti * 128:(ti + 1) * 128], pb[b][:, 0:128]),
                          reads=[K("pb", b)], writes=[K("kT", ti)])
                    mk.op("vector", lambda v, ti=ti: v.tensor_copy(vt[:, ti * 128:(ti + 1) * 128], R_KV[:, 128:256]),
                          reads=[K("kv")], writes=[K("vt", ti)])
    mk_barrier(mk)

    if sb.get("dbgq") is not None:
        mk.dma("gpsimd", lambda q: q.dma_start(out=sb["dbgq"][:, 0:H8 * NT * 128], in_=A1[:, 32768:32768 + H8 * NT * 128]),
               reads=[K("qT", h, ti) for h in range(H8) for ti in range(NT)], writes=["dbq"])
        mk.dma("gpsimd", lambda q: q.dma_start(out=sb["dbgq"][:, H8 * NT * 128:(H8 + 1) * NT * 128], in_=kT[:, 0:NT * 128]),
               reads=[K("kT", ti) for ti in range(NT)], writes=["dbk"])
        mk.dma("gpsimd", lambda q: q.dma_start(out=sb["dbgq"][:, (H8 + 1) * NT * 128:(H8 + 2) * NT * 128], in_=vt[:, 0:NT * 128]),
               reads=[K("vt", ti) for ti in range(NT)], writes=["dbv"])
        if sb.get("dbgcs") is not None:
            mk.dma("gpsimd", lambda q: q.dma_start(out=sb["dbgcs"], in_=cs[:, 0:2 * NT * 64]),
                   reads=[K("cs", 0), K("cs", 1)], writes=["dbcs"])
    smc = mk.const(SM)
    NKB = NT * 128 // 512
    for h in range(H8):
        for ti in range(NT):
            st = lambda c, ti=ti: stats[:, ti * SC + c: ti * SC + c + 1]
            qcol = 32768 + (h * NT + ti) * 128
            for kb in range(NKB):
                mk.op("tensor", lambda t, kb=kb, qcol=qcol: t.matmul(
                    pb[kb][:, 0:512], A1[:, qcol:qcol + 128], kT[:, kb * 512:(kb + 1) * 512],
                    start=True, stop=True),
                    reads=[K("qT", h, ti)] + [K("kT", t4) for t4 in range(kb * 4, kb * 4 + 4)],
                    writes=[K("pb", kb)])
            for kb in range(NKB):
                mk.op("vector", lambda v, kb=kb: v.reduce_max(zc[:, kb:kb + 1], pb[kb][:, 0:512], axis=AX.X),
                      reads=[K("pb", kb)], writes=[K("rm", kb)])
            mk.op("vector", lambda v: v.reduce_max(zc[:, 4:5], zc[:, 0:NKB], axis=AX.X),
                  reads=[K("rm", kb2) for kb2 in range(NKB)], writes=[K("rm4")])
            mk.op("vector", lambda v: v.tensor_scalar(zc[:, 4:5], zc[:, 4:5], -SM, None, ALU.mult),
                  reads=[K("rm4")], writes=[K("rm4")])
            for kb in range(NKB):
                mk.op("scalar", lambda s, kb=kb, smc=smc: s.activation(
                    R_AT[:, kb * 512:(kb + 1) * 512], pb[kb][:, 0:512], AF.Exp,
                    bias=zc[:, 4:5], scale=smc, accum_out=zc[:, kb:kb + 1]),
                    reads=[K("pb", kb), K("rm4")] + mk.cdep(SM), writes=[K("at", kb), K("atn"), K("rm", kb)])
            mk.op("vector", lambda v: v.reduce_sum(zc[:, 5:6], zc[:, 0:NKB], axis=AX.X),
                  reads=[K("rm", kb2) for kb2 in range(NKB)], writes=[K("z")])
            mk.op("vector", lambda v: v.reciprocal(zc[:, 5:6], zc[:, 5:6]),
                  reads=[K("z")], writes=[K("z")])
            mk.op("vector", lambda v: v.tensor_scalar(R_AT, R_AT, zc[:, 5:6], None, ALU.mult),
                  reads=[K("z")] + [K("at", kb2) for kb2 in range(NKB)], writes=[K("atn")])
            for kc in range(NT):
                b = 6 + kc % 2
                mk.op("tensor", lambda t, kc=kc, b=b: t.transpose(pb[b][:, 0:128], R_AT[:, kc * 128:(kc + 1) * 128], idt[:]),
                      reads=[K("atn"), "iden"], writes=[K("pb", b)])
                col = (kc * NT + ti) * 128
                mk.op("vector", lambda v, col=col, b=b: v.tensor_copy(A1[:, col:col + 128], pb[b][:, 0:128]),
                      reads=[K("pb", b)], writes=[K("aT", kc, ti)])
        for qb in range(NT * 128 // 512):
            bb_ = 4 + qb % 2
            for kc in range(NT):
                mk.op("tensor", lambda t, qb=qb, kc=kc, bb_=bb_: t.matmul(
                    pb[bb_][:, 0:512], vt[:, kc * 128:(kc + 1) * 128],
                    A1[:, kc * NT * 128 + qb * 512: kc * NT * 128 + qb * 512 + 512],
                    start=(kc == 0), stop=(kc == NT - 1)),
                    reads=[K("vt", kc)] + [K("aT", kc, t4) for t4 in range(qb * 4, qb * 4 + 4)],
                    writes=[K("pb", bb_)])
            col = 32768 + h * NT * 128 + qb * 512
            mk.op("vector", lambda v, col=col, bb_=bb_: v.tensor_copy(A1[:, col:col + 512], pb[bb_][:, 0:512]),
                  reads=[K("pb", bb_)],
                  writes=[K("oT", h, qb)] + [K("qT", h, t4) for t4 in range(qb * 4, qb * 4 + 4)])
    mk_barrier(mk)

    for ob in range(4):
        for hk in range(2):
            par = (ob * 2 + hk) % 2
            wslot = sb["wob"][:, par * 2048: par * 2048 + 4 * 512]
            wk = K("wob", par)
            mk.dma("gpsimd", lambda q, ob=ob, wslot=wslot, hk=hk: q.dma_start(
                out=wslot.rearrange("p (kc n) -> p kc n", kc=4),
                in_=woT.rearrange("(kc p) d -> p kc d", p=128)[:, hk * 4:(hk + 1) * 4, ob * 512:(ob + 1) * 512]),
                writes=[wk])
        for tig in range(0, NT, 4):
            tis = list(range(tig, min(tig + 4, NT)))
            for kc in range(8):
                par = (ob * 2 + kc // 4) % 2
                wslot = sb["wob"][:, par * 2048: par * 2048 + 4 * 512]
                wk = K("wob", par)
                for j, ti in enumerate(tis):
                    col = 32768 + kc * NT * 128 + ti * 128
                    mk.op("tensor", lambda t, j=j, col=col, wslot=wslot, kc=kc: t.matmul(
                        pb[j][:, 0:512], A1[:, col:col + 128], wslot[:, (kc % 4) * 512:(kc % 4 + 1) * 512],
                        start=(kc == 0), stop=(kc == 7)),
                        reads=[K("oT", kc, ti * 128 // 512), wk], writes=[K("pb", j)])
            for j, ti in enumerate(tis):
                ev = sb["evo"][:, (j % 2) * 512:(j % 2) * 512 + 512]
                mk.op("vector", lambda v, j=j, ev=ev: v.tensor_copy(ev, pb[j][:, 0:512]),
                      reads=[K("pb", j)], writes=[K("evo", j % 2)])
                mk.dma("scalar", lambda q, ti=ti, ob=ob, ev=ev: q.dma_start(
                    out=pc_out[ti * P:(ti + 1) * P, ob * 512:(ob + 1) * 512], in_=ev),
                    reads=[K("evo", j % 2)], writes=[K("pc", ti, ob)])
    mk_barrier(mk)
    return [K("pc", ti, ob) for ti in range(NT) for ob in range(4)]







def _bcast4(mk, K, bankkey, pbank, ones1x4, src_row, src_key, dst, dst_key, width):
    nmm = (width + 511) // 512
    for i in range(nmm):
        w = min(512, width - i * 512)
        mk.op("tensor", lambda t, i=i, w=w: t.matmul(
            pbank[0:4, 0:w], ones1x4[0:1, 0:4], src_row[0:1, i * 512:i * 512 + w],
            start=True, stop=True),
            reads=[src_key, "ones1x4"], writes=[(bankkey,)])
        mk.op("vector", lambda v, i=i, w=w: v.tensor_copy(dst[0:4, i * 512:i * 512 + w], pbank[0:4, 0:w]),
              reads=[(bankkey,)], writes=[dst_key])


def emit_thought_iter(mk, tag, agg, x_or_t_in, partials_in, d1s_ap, d1b_ap, d2s_ap,
                      png_ap, pnb_ap, out_partial, out_thought,
                      agg_w_ap=None, agg_b_ap=None, ing_ap=None, inb_ap=None,
                      d2b_ap=None, iden_ap=None, sb=None):
    K = lambda *a: (tag,) + a
    P4 = 4
    A = sb["tA"]
    B = sb["tB"]
    H = sb["tH"]
    R4 = sb["tR"]
    st4 = sb["tst"]
    row = sb["trow"]
    ones1x4 = sb["ones1x4"]
    onesb = sb["onesb"]
    tT = sb["tT"]
    w1 = sb["w1"]
    w2 = sb["w2"]
    br = sb["brow"]
    pb = sb["pb"]
    idt = sb["idt"]

    if not sb.get("_init"):
        sb["_init"] = True
        mk.dma("sync", lambda q: q.dma_start(out=idt[:], in_=iden_ap), writes=["iden"])
        mk.op("vector", lambda v: v.memset(ones1x4[:], 1.0), writes=["ones1x4"])
        mk.op("vector", lambda v: v.memset(onesb[:], 1.0), writes=["onesb"])

    mk.dma("gpsimd", lambda q: q.dma_start(out=w1[:].rearrange("p (kc n) -> p kc n", kc=16),
                                           in_=d1s_ap.rearrange("(kc p) d -> p kc d", p=128)),
           writes=[K("w1")])
    mk.dma("gpsimd", lambda q: q.dma_start(out=w2[:].rearrange("p (kc n) -> p kc n", kc=8),
                                           in_=d2s_ap.rearrange("(kc p) d -> p kc d", p=128)),
           writes=[K("w2")])
    mk.dma("gpsimd", lambda q: q.dma_start(out=br[0:1, 0:1024], in_=d1b_ap), writes=[K("d1b")])

    if agg:
        mk.dma("sync", lambda q: q.dma_start(out=A[0:4, :], in_=x_or_t_in), writes=[K("A")])
        for pi, pap in enumerate(partials_in):
            mk.dma("sync", lambda q, pap=pap: q.dma_start(out=R4[0:4, :], in_=pap), writes=[K("R4")])
            mk.op("vector", lambda v: v.tensor_add(A[0:4, :], A[0:4, :], R4[0:4, :]),
                  reads=[K("A"), K("R4")], writes=[K("A")])
        wagg = sb["wagg"]
        mk.dma("gpsimd", lambda q: q.dma_start(out=wagg[:].rearrange("p (kc n) -> p kc n", kc=16),
                                               in_=agg_w_ap.rearrange("(kc p) d -> p kc d", p=128)),
               writes=[K("wagg")])
        mk.dma("gpsimd", lambda q: q.dma_start(out=sb["brow2"][0:1, :], in_=agg_b_ap), writes=[K("aggb")])
        emit_rowln(mk, K, "pln", A, K("A"), B, K("B"), st4, None, None, 1e-5, sb)
        emit_rowln(mk, K, "aln", B, K("B"), A, K("A"), st4, ing_ap, inb_ap, 1e-6, sb)
        cur, curk = A, K("A")
    else:
        mk.dma("sync", lambda q: q.dma_start(out=A[0:4, :], in_=x_or_t_in), writes=[K("A")])
        for pi, pap in enumerate(partials_in):
            mk.dma("sync", lambda q, pap=pap: q.dma_start(out=R4[0:4, :], in_=pap), writes=[K("R4")])
            mk.op("vector", lambda v: v.tensor_add(A[0:4, :], A[0:4, :], R4[0:4, :]),
                  reads=[K("A"), K("R4")], writes=[K("A")])
        mk.dma("sync", lambda q: q.dma_start(out=row[0:1, :], in_=d2b_ap), writes=[("trow",)])
        _bcast4(mk, K, "pbk6", pb[6], ones1x4, row, ("trow",), R4, K("R4"), 2048)
        mk.op("vector", lambda v: v.tensor_add(A[0:4, :], A[0:4, :], R4[0:4, :]),
              reads=[K("A"), K("R4")], writes=[K("A")])
        cur, curk = A, K("A")
    if not agg:
        mk.dma("scalar", lambda q: q.dma_start(out=out_thought, in_=cur[0:4, :]),
               reads=[curk], writes=[K("outT")])

    if agg:
        emit_row_transpose(mk, K, cur, curk, tT, K("tT"), idt, pb, sb, 16)
        wagg = sb["wagg"]
        for ob in range(4):
            for kc in range(16):
                mk.op("tensor", lambda t, ob=ob, kc=kc: t.matmul(
                    pb[0][0:4, 0:512], tT[:, kc * 4:(kc + 1) * 4], wagg[:, kc * 2048 + ob * 512: kc * 2048 + (ob + 1) * 512],
                    start=(kc == 0), stop=False),
                    reads=[K("tT"), K("wagg")], writes=[K("pbt")])
            mk.op("tensor", lambda t, ob=ob: t.matmul(
                pb[0][0:4, 0:512], onesb[0:1, 0:4], sb["brow2"][0:1, ob * 512:(ob + 1) * 512],
                start=False, stop=True),
                reads=[K("aggb"), "onesb"], writes=[K("pbt")])
            mk.op("vector", lambda v, ob=ob: v.tensor_copy(A[0:4, ob * 512:(ob + 1) * 512], pb[0][0:4, 0:512]),
                  reads=[K("pbt")], writes=[K("A")])
        cur, curk = A, K("A")
        mk.dma("scalar", lambda q: q.dma_start(out=out_thought, in_=cur[0:4, :]),
               reads=[curk], writes=[K("outT")])

    emit_rowln(mk, K, "pnl", cur, curk, B, K("B"), st4, png_ap, pnb_ap, 1e-6, sb)
    emit_row_transpose(mk, K, B, K("B"), tT, K("tT"), idt, pb, sb, 16)
    for ob in range(2):
        for kc in range(16):
            mk.op("tensor", lambda t, ob=ob, kc=kc: t.matmul(
                pb[0][0:4, 0:512], tT[:, kc * 4:(kc + 1) * 4], w1[:, kc * 1024 + ob * 512: kc * 1024 + (ob + 1) * 512],
                start=(kc == 0), stop=False),
                reads=[K("tT"), K("w1")], writes=[K("pbt")])
        mk.op("tensor", lambda t, ob=ob: t.matmul(
            pb[0][0:4, 0:512], onesb[0:1, 0:4], br[0:1, ob * 512:(ob + 1) * 512],
            start=False, stop=True),
            reads=[K("d1b"), "onesb"], writes=[K("pbt")])
        mk.op("scalar", lambda s, ob=ob: s.activation(H[0:4, ob * 512:(ob + 1) * 512], pb[0][0:4, 0:512], AF.Gelu),
              reads=[K("pbt")], writes=[K("H")])
    emit_row_transpose(mk, K, H, K("H"), tT, K("tT"), idt, pb, sb, 8)
    for ob in range(4):
        for kc in range(8):
            mk.op("tensor", lambda t, ob=ob, kc=kc: t.matmul(
                pb[1][0:4, 0:512], tT[:, kc * 4:(kc + 1) * 4], w2[:, kc * 2048 + ob * 512: kc * 2048 + (ob + 1) * 512],
                start=(kc == 0), stop=(kc == 7)),
                reads=[K("tT"), K("w2")], writes=[K("pb2")])
        mk.op("vector", lambda v, ob=ob: v.tensor_copy(B[0:4, ob * 512:(ob + 1) * 512], pb[1][0:4, 0:512]),
              reads=[K("pb2")], writes=[K("B")])
    mk.dma("scalar", lambda q: q.dma_start(out=out_partial, in_=B[0:4, :]),
           reads=[K("B")], writes=[K("outP")])
    return [K("outP"), K("outT")]


def emit_rowln(mk, K, nm, src, srck, dst, dstk, st4, g_ap, b_ap, eps, sb):
    mk.op("vector", lambda v: v.reduce_sum(st4[0:4, 0:1], src[0:4, :], axis=AX.X),
          reads=[srck], writes=[K(nm, 0)])
    mk.op("vector", lambda v: v.tensor_scalar(st4[0:4, 1:2], st4[0:4, 0:1], -1.0 / 2048.0, None, ALU.mult),
          reads=[K(nm, 0)], writes=[K(nm, 1)])
    mk.op("vector", lambda v: v.tensor_scalar(dst[0:4, :], src[0:4, :], st4[0:4, 1:2], None, ALU.add),
          reads=[srck, K(nm, 1)], writes=[dstk])
    mk.op("scalar", lambda s: s.activation(sb["tsq"][0:4, :], dst[0:4, :], AF.Square, accum_out=st4[0:4, 2:3]),
          reads=[dstk], writes=[K(nm, "sq"), K(nm, 2)])
    mk.op("vector", lambda v, eps=eps: v.tensor_scalar(st4[0:4, 2:3], st4[0:4, 2:3], 1.0 / 2048.0, eps, ALU.mult, ALU.add),
          reads=[K(nm, 2)], writes=[K(nm, 2)])
    mk.op("scalar", lambda s: s.sqrt(st4[0:4, 2:3], st4[0:4, 2:3]), reads=[K(nm, 2)], writes=[K(nm, 2)])
    mk.op("vector", lambda v: v.reciprocal(st4[0:4, 3:4], st4[0:4, 2:3]), reads=[K(nm, 2)], writes=[K(nm, 3)])
    mk.op("vector", lambda v: v.tensor_scalar(dst[0:4, :], dst[0:4, :], st4[0:4, 3:4], None, ALU.mult),
          reads=[dstk, K(nm, 3)], writes=[dstk])
    if g_ap is not None:
        row, R4, pb, ones1x4 = sb["trow"], sb["tR"], sb["pb"], sb["ones1x4"]
        mk.dma("sync", lambda q: q.dma_start(out=row[0:1, :], in_=g_ap), writes=[("trow",)])
        _bcast4(mk, K, "pbk7", pb[7], ones1x4, row, ("trow",), R4, K("R4"), 2048)
        mk.op("vector", lambda v: v.tensor_mul(dst[0:4, :], dst[0:4, :], R4[0:4, :]),
              reads=[dstk, K("R4")], writes=[dstk])
        mk.dma("sync", lambda q: q.dma_start(out=row[0:1, :], in_=b_ap), writes=[("trow",)])
        _bcast4(mk, K, "pbk7", pb[7], ones1x4, row, ("trow",), R4, K("R4"), 2048)
        mk.op("vector", lambda v: v.tensor_add(dst[0:4, :], dst[0:4, :], R4[0:4, :]),
              reads=[dstk, K("R4")], writes=[dstk])


def emit_row_transpose(mk, K, src, srck, dstT, dstk, idt, pb, sb, nblk):
    for kc in range(nblk):
        b = 4 + kc % 2
        mk.op("tensor", lambda t, kc=kc, b=b: t.transpose(pb[b][0:128, 0:4], src[0:4, kc * 128:(kc + 1) * 128], idt[0:4, 0:4]),
              reads=[srck, "iden"], writes=[K("pbtr", b)])
        mk.op("vector", lambda v, kc=kc, b=b: v.tensor_copy(dstT[:, kc * 4:(kc + 1) * 4], pb[b][0:128, 0:4]),
              reads=[K("pbtr", b)], writes=[dstk])


def alloc_thought_base(mk):
    sb = {}
    sb["pb"] = [mk.psum(f"pb{i}", [128, 512], F32) for i in range(8)]
    sb["idt"] = mk.sbuf("idt", [128, 128], F32)
    sb["tA"] = mk.sbuf("tA", [4, 2048], F32)
    sb["tB"] = mk.sbuf("tB", [4, 2048], F32)
    sb["tH"] = mk.sbuf("tH", [4, 1024], F32)
    sb["tR"] = mk.sbuf("tR", [4, 2048], F32)
    sb["tsq"] = mk.sbuf("tsq", [4, 2048], F32)
    sb["tst"] = mk.sbuf("tst", [4, 8], F32)
    sb["trow"] = mk.sbuf("trow", [1, 2048], F32)
    sb["brow"] = mk.sbuf("brow", [1, 2048], BF16)
    sb["brow2"] = mk.sbuf("brow2", [1, 2048], BF16)
    sb["ones1x4"] = mk.sbuf("ones1x4", [1, 4], F32)
    sb["onesb"] = mk.sbuf("onesb", [1, 4], BF16)
    sb["tT"] = mk.sbuf("tT", [128, 64], BF16)
    return sb


def alloc_thought_sb(mk):
    sb = alloc_thought_base(mk)
    sb["w1"] = mk.sbuf("w1", [128, 16 * 1024], BF16)
    sb["w2"] = mk.sbuf("w2", [128, 8 * 2048], BF16)
    sb["wagg"] = mk.sbuf("wagg", [128, 16 * 2048], BF16)
    return sb

def alloc_ffn1z_sb(mk):
    sb = alloc_thought_base(mk)
    sb["wbig"] = mk.sbuf("wbig", [128, 16 * 2048], BF16)
    sb["bv"] = mk.sbuf("bv", [128, 2048], F32)
    sb["gb"] = mk.sbuf("gb", [128, 2048], F32)
    sb["bb"] = mk.sbuf("bb2", [128, 2048], F32)
    sb["xp"] = mk.sbuf("xp", [128, 2048], F32)
    sb["xo"] = mk.sbuf("xo", [128, 2048], F32)
    sb["xg"] = mk.sbuf("xg", [128, 2048], F32)
    sb["xpT"] = mk.sbuf("xpT", [128, 16 * 128], BF16)
    sb["stats2"] = mk.sbuf("stats2", [128, 16], F32)
    sb["ones_row"] = mk.sbuf("ones_row", [1, 128], F32)
    sb["bvrow"] = mk.sbuf("bvrow", [1, 2048], F32)
    sb["onesbr"] = mk.sbuf("onesbr", [1, 128], BF16)
    sb["selb"] = mk.sbuf("selb_sb", [4, 1], F32)
    return sb


def emit_ffn1_final(mk, tag, T, selb_ap, x_in, thought_in, partials_in, d2b_ap,
                    bcw_ap, bcb_ap, gw_ap, gwb_ap, og_ap, ob_ap, x_out, iden_ap, sb,
                    x_extras=()):
    P = 128
    NT = T // P
    K = lambda *a: (tag,) + a
    A, R4, row = sb["tA"], sb["tR"], sb["trow"]
    st4, tT, pb, idt = sb["tst"], sb["tT"], sb["pb"], sb["idt"]
    ones1x4, onesb = sb["ones1x4"], sb["onesb"]
    wbig, bv, gbt, bbt = sb["wbig"], sb["bv"], sb["gb"], sb["bb"]
    xp, xo, xg, xpT = sb["xp"], sb["xo"], sb["xg"], sb["xpT"]
    st = lambda c: sb["stats2"][:, c:c + 1]

    if not sb.get("_init"):
        sb["_init"] = True
        mk.dma("sync", lambda q: q.dma_start(out=idt[:], in_=iden_ap), writes=["iden"])
        mk.op("vector", lambda v: v.memset(ones1x4[:], 1.0), writes=["ones1x4"])
        mk.op("vector", lambda v: v.memset(onesb[:], 1.0), writes=["onesb"])
        mk.op("vector", lambda v: v.memset(sb["ones_row"][:], 1.0), writes=["ones_row"])
        mk.op("vector", lambda v: v.memset(sb["onesbr"][:], 1.0), writes=["onesbr"])

    mk.dma("sync", lambda q: q.dma_start(out=A[0:4, :], in_=thought_in), writes=[K("A")])
    for pap in partials_in:
        mk.dma("sync", lambda q, pap=pap: q.dma_start(out=R4[0:4, :], in_=pap), writes=[K("R4")])
        mk.op("vector", lambda v: v.tensor_add(A[0:4, :], A[0:4, :], R4[0:4, :]),
              reads=[K("A"), K("R4")], writes=[K("A")])
    mk.dma("sync", lambda q: q.dma_start(out=row[0:1, :], in_=d2b_ap), writes=[("trow",)])
    _bcast4(mk, K, K("pbk6"), pb[6], ones1x4, row, ("trow",), R4, K("R4"), 2048)
    mk.op("vector", lambda v: v.tensor_add(A[0:4, :], A[0:4, :], R4[0:4, :]),
          reads=[K("A"), K("R4")], writes=[K("A")])
    mk.dma("sync", lambda q: q.dma_start(out=sb["selb"][0:4, 0:1], in_=selb_ap), writes=[K("selb")])
    for obk in range(4):
        mk.op("tensor", lambda t, obk=obk: t.matmul(
            pb[2][0:1, 0:512], sb["selb"][0:4, 0:1], A[0:4, obk * 512:(obk + 1) * 512],
            start=True, stop=True),
            reads=[K("selb"), K("A")], writes=[K("pbv")])
        mk.op("vector", lambda v, obk=obk: v.tensor_copy(row[0:1, obk * 512:(obk + 1) * 512], pb[2][0:1, 0:512]),
              reads=[K("pbv")], writes=[("trow",)])
    for kc in range(16):
        b = 4 + kc % 2
        mk.op("tensor", lambda t, kc=kc, b=b: t.transpose(pb[b][0:128, 0:1], row[0:1, kc * 128:(kc + 1) * 128], idt[0:1, 0:1]),
              reads=[("trow",), "iden"], writes=[K("pbtr", b)])
        mk.op("vector", lambda v, kc=kc, b=b: v.tensor_copy(tT[:, kc:kc + 1], pb[b][0:128, 0:1]),
              reads=[K("pbtr", b)], writes=[K("tT")])
    mk.dma("gpsimd", lambda q: q.dma_start(out=wbig[:].rearrange("p (kc n) -> p kc n", kc=16),
                                           in_=bcw_ap.rearrange("(kc p) d -> p kc d", p=128)),
           writes=[K("wbig")])
    mk.dma("gpsimd", lambda q: q.dma_start(out=sb["brow2"][0:1, :], in_=bcb_ap), writes=[K("bcb")])
    for obk in range(4):
        for kc in range(16):
            mk.op("tensor", lambda t, obk=obk, kc=kc: t.matmul(
                pb[2][0:1, 0:512], tT[:, kc:kc + 1],
                wbig[:, kc * 2048 + obk * 512: kc * 2048 + (obk + 1) * 512],
                start=(kc == 0), stop=False),
                reads=[K("tT"), K("wbig")], writes=[K("pbv")])
        mk.op("tensor", lambda t, obk=obk: t.matmul(
            pb[2][0:1, 0:512], onesb[0:1, 0:1], sb["brow2"][0:1, obk * 512:(obk + 1) * 512],
            start=False, stop=True),
            reads=[K("bcb"), "onesb"], writes=[K("pbv")])
        mk.op("vector", lambda v, obk=obk: v.tensor_copy(sb["bvrow"][0:1, obk * 512:(obk + 1) * 512], pb[2][0:1, 0:512]),
              reads=[K("pbv")], writes=[K("bvrow")])
    for obk in range(4):
        mk.op("tensor", lambda t, obk=obk: t.matmul(
            pb[3][:, 0:512], sb["ones_row"][0:1, :], sb["bvrow"][0:1, obk * 512:(obk + 1) * 512],
            start=True, stop=True),
            reads=[K("bvrow"), "ones_row"], writes=[K("pbb")])
        mk.op("vector", lambda v, obk=obk: v.tensor_copy(bv[:, obk * 512:(obk + 1) * 512], pb[3][:, 0:512]),
              reads=[K("pbb")], writes=[K("bv")])
    for nm, ap_, dst, dk in [("og", og_ap, gbt, K("gbt")), ("ob", ob_ap, bbt, K("bbt"))]:
        mk.dma("sync", lambda q, ap_=ap_: q.dma_start(out=row[0:1, :], in_=ap_), writes=[("trow",)])
        for obk in range(4):
            mk.op("tensor", lambda t, obk=obk, nm=nm: t.matmul(
                pb[3][:, 0:512], sb["ones_row"][0:1, :], row[0:1, obk * 512:(obk + 1) * 512],
                start=True, stop=True),
                reads=[("trow",), "ones_row"], writes=[K("pbb")])
            mk.op("vector", lambda v, obk=obk, dst=dst: v.tensor_copy(dst[:, obk * 512:(obk + 1) * 512], pb[3][:, 0:512]),
                  reads=[K("pbb")], writes=[dk])
    mk.dma("gpsimd", lambda q: q.dma_start(out=wbig[:].rearrange("p (kc n) -> p kc n", kc=16),
                                           in_=gw_ap.rearrange("(kc p) d -> p kc d", p=128)),
           writes=[K("wbig")])
    mk.dma("gpsimd", lambda q: q.dma_start(out=sb["brow"][0:1, :], in_=gwb_ap), writes=[K("gwb")])
    mk_barrier(mk)

    for ti in range(NT):
        xk = K("xt")
        mk.dma("sync", lambda q, ti=ti: q.dma_start(out=xo[:], in_=x_in[ti * P:(ti + 1) * P, :]), writes=[xk])
        for eap in x_extras:
            mk.dma("sync", lambda q, ti=ti, eap=eap: q.dma_start(out=xg[:], in_=eap[ti * P:(ti + 1) * P, :]),
                   writes=[K("xg"), K("xe")])
            mk.op("vector", lambda v: v.tensor_add(xo[:], xo[:], xg[:]),
                  reads=[xk, K("xe")], writes=[xk])
        mk.op("vector", lambda v: v.reduce_sum(st(0), xo[:], axis=AX.X), reads=[xk], writes=[K("s0")])
        mk.op("vector", lambda v: v.tensor_scalar(st(1), st(0), -1.0 / 2048.0, None, ALU.mult),
              reads=[K("s0")], writes=[K("s1")])
        mk.op("vector", lambda v: v.tensor_scalar(xp[:], xo[:], st(1), None, ALU.add),
              reads=[xk, K("s1")], writes=[K("xp")])
        mk.op("scalar", lambda s: s.activation(xg[:], xp[:], AF.Square, accum_out=st(2)),
              reads=[K("xp")], writes=[K("xg"), K("s2")])
        mk.op("vector", lambda v: v.tensor_scalar(st(2), st(2), 1.0 / 2048.0, 1e-5, ALU.mult, ALU.add),
              reads=[K("s2")], writes=[K("s2")])
        mk.op("scalar", lambda s: s.sqrt(st(2), st(2)), reads=[K("s2")], writes=[K("s2")])
        mk.op("vector", lambda v: v.reciprocal(st(3), st(2)), reads=[K("s2")], writes=[K("s3")])
        mk.op("vector", lambda v: v.tensor_scalar(xp[:], xp[:], st(3), None, ALU.mult),
              reads=[K("xp"), K("s3")], writes=[K("xp")])
        for kc in range(16):
            b = 4 + kc % 2
            mk.op("tensor", lambda t, kc=kc, b=b: t.transpose(pb[b][:, 0:128], xp[:, kc * 128:(kc + 1) * 128], idt[:]),
                  reads=[K("xp"), "iden"], writes=[K("pbtr", b)])
            mk.op("vector", lambda v, kc=kc, b=b: v.tensor_copy(xpT[:, kc * 128:(kc + 1) * 128], pb[b][:, 0:128]),
                  reads=[K("pbtr", b)], writes=[K("xpT")])
        for obk in range(4):
            bank = obk
            for kc in range(16):
                mk.op("tensor", lambda t, obk=obk, kc=kc, bank=bank: t.matmul(
                    pb[bank][:, 0:512], xpT[:, kc * 128:(kc + 1) * 128],
                    wbig[:, kc * 2048 + obk * 512: kc * 2048 + (obk + 1) * 512],
                    start=(kc == 0), stop=False),
                    reads=[K("xpT"), K("wbig")], writes=[K("pbg", bank)])
            mk.op("tensor", lambda t, obk=obk, bank=bank: t.matmul(
                pb[bank][:, 0:512], sb["onesbr"][0:1, 0:128],
                sb["brow"][0:1, obk * 512:(obk + 1) * 512],
                start=False, stop=True),
                reads=[K("gwb"), "onesbr"], writes=[K("pbg", bank)])
            mk.op("scalar", lambda s, obk=obk, bank=bank: s.activation(xg[:, obk * 512:(obk + 1) * 512], pb[bank][:, 0:512], AF.Sigmoid),
                  reads=[K("pbg", bank)], writes=[K("xg")])
        mk.op("vector", lambda v: v.tensor_mul(xg[:], xg[:], bv[:]),
              reads=[K("xg"), K("bv")], writes=[K("xg")])
        mk.op("vector", lambda v: v.tensor_add(xg[:], xg[:], xp[:]),
              reads=[K("xg"), K("xp")], writes=[K("xg")])
        mk.op("vector", lambda v: v.reduce_sum(st(4), xg[:], axis=AX.X), reads=[K("xg")], writes=[K("s4")])
        mk.op("vector", lambda v: v.tensor_scalar(st(5), st(4), -1.0 / 2048.0, None, ALU.mult),
              reads=[K("s4")], writes=[K("s5")])
        mk.op("vector", lambda v: v.tensor_scalar(xp[:], xg[:], st(5), None, ALU.add),
              reads=[K("xg"), K("s5")], writes=[K("xp")])
        mk.op("scalar", lambda s: s.activation(xg[:], xp[:], AF.Square, accum_out=st(6)),
              reads=[K("xp")], writes=[K("xg"), K("s6")])
        mk.op("vector", lambda v: v.tensor_scalar(st(6), st(6), 1.0 / 2048.0, 1e-6, ALU.mult, ALU.add),
              reads=[K("s6")], writes=[K("s6")])
        mk.op("scalar", lambda s: s.sqrt(st(6), st(6)), reads=[K("s6")], writes=[K("s6")])
        mk.op("vector", lambda v: v.reciprocal(st(7), st(6)), reads=[K("s6")], writes=[K("s7")])
        mk.op("vector", lambda v: v.tensor_scalar(xp[:], xp[:], st(7), None, ALU.mult),
              reads=[K("xp"), K("s7")], writes=[K("xp")])
        mk.op("vector", lambda v: v.tensor_mul(xp[:], xp[:], gbt[:]),
              reads=[K("xp"), K("gbt")], writes=[K("xp")])
        mk.op("vector", lambda v: v.tensor_add(xp[:], xp[:], bbt[:]),
              reads=[K("xp"), K("bbt")], writes=[K("xp")])
        mk.op("vector", lambda v: v.scalar_tensor_tensor(xo[:], xp[:], 0.5, xo[:], ALU.mult, ALU.add),
              reads=[K("xp"), xk], writes=[xk])
        mk.dma("scalar", lambda q, ti=ti: q.dma_start(out=x_out[ti * P:(ti + 1) * P, :], in_=xo[:]),
               reads=[xk], writes=[K("xout", ti)])
    return [K("xout", ti) for ti in range(NT)]



def alloc_att0_sb(mk, NT):
    sb = {}
    sb["A1"] = mk.sbuf("A1", [128, 65536], BF16)
    sb["A2"] = mk.sbuf("A2", [128, 4352], F32)
    sb["pb"] = [mk.psum(f"pb{i}", [128, 512], F32) for i in range(8)]
    sb["idt"] = mk.sbuf("idt", [128, 128], F32)
    sb["stats"] = mk.sbuf("stats", [128, NT * 16], F32)
    sb["ones_col"] = mk.sbuf("ones_col", [128, 1], F32)
    sb["ones_row"] = mk.sbuf("ones_row", [1, 128], F32)
    sb["wst"] = mk.sbuf("wst", [128, 2048], BF16)
    sb["ba"] = mk.sbuf("ba", [128, NT * 32], F32)
    sb["gb"] = mk.sbuf("gb", [128, NT * 16], F32)
    sb["bb"] = mk.sbuf("bb", [128, NT * 16], F32)
    sb["gcum"] = mk.sbuf("gcum", [128, NT * 16], F32)
    sb["eaa"] = mk.sbuf("eaa", [128, NT * 16], F32)
    sb["ebb"] = mk.sbuf("ebb", [128, NT * 16], F32)
    sb["eww"] = mk.sbuf("eww", [128, NT * 16], F32)
    sb["beb"] = mk.sbuf("beb", [128, NT * 16], F32)
    sb["ebl"] = mk.sbuf("ebl", [128, 2 * NT * 16], F32)
    sb["r16"] = mk.sbuf("r16", [128, 64], F32)
    sb["rrb"] = mk.sbuf("rrb", [128, 2048], BF16)
    sb["cw"] = mk.sbuf("cw", [128, 128], F32)
    sb["cb"] = mk.sbuf("cb", [128, 32], F32)
    sb["ktk"] = mk.sbuf("ktk", [128, NT * 128], BF16)
    sb["vtk"] = mk.sbuf("vtk", [128, NT * 128], BF16)
    sb["dw"] = mk.sbuf("dw", [128, 1024], BF16)
    sb["dw2"] = mk.sbuf("dw2", [128, 1024], BF16)
    sb["df"] = mk.sbuf("df", [128, 512], F32)
    sb["Sst"] = mk.sbuf("Sst", [128, 128], F32)
    sb["Sbf"] = mk.sbuf("Sbf", [128, 128], BF16)
    sb["zt"] = mk.sbuf("zt", [128, 2048], BF16)
    sb["nwb"] = mk.sbuf("nwb", [128, 128], F32)
    sb["goutb"] = mk.sbuf("goutb", [128, 2048], BF16)
    sb["gT"] = mk.sbuf("gT", [128, 2048], BF16)
    sb["evo"] = mk.sbuf("evo", [128, 512], F32)
    sb["wbab"] = mk.sbuf("wbab", [128, 512], BF16)
    sb["knbT"] = mk.sbuf("knbT", [128, NT * 128], BF16)
    sb["cstg"] = sb["A2"][:, 0:2051]
    sb["glx"] = sb["A2"][:, 0:1024]
    sb["cacc"] = sb["A2"][:, 2052:4100]
    sb["csil"] = sb["goutb"]
    sb["rr"] = sb["A2"][:, 2052:4100]
    return sb


def emit_att0(mk, tag, NT, x_in, wqkvz, wba, convw_r, convb_r, alog, dtb, nw_ap,
              wout, z_hbm, qk_hbm, pc_out, iden_ap, idb_ap, tri_ap, msk_ap, sb):
    P = 128
    T = NT * P
    NC = 2 * NT
    NKC = 16
    A1, A2, pb = sb["A1"], sb["A2"], sb["pb"]
    idt, stats = sb["idt"], sb["stats"]
    SC = 16
    K = lambda *a: (tag,) + a
    VBASE = 16 * NT * 128

    idb = sb.get("idb")
    if idb is None:
        idb = sb["idb"] = mk.sbuf("idb_sb", [64, 64], BF16)
    tri = sb.get("tri")
    if tri is None:
        tri = sb["tri"] = mk.sbuf("tri_sb", [128, 387], F32)
    msk = sb.get("msk")
    if msk is None:
        msk = sb["msk"] = mk.sbuf("msk_sb", [64, 256], F32)
    if not sb.get("_init"):
        sb["_init"] = True
        mk.dma("sync", lambda q: q.dma_start(out=idt[:], in_=iden_ap), writes=["iden"])
        mk.dma("sync", lambda q: q.dma_start(out=idb[:], in_=idb_ap), writes=["idb"])
        mk.dma("sync", lambda q: q.dma_start(out=tri[:], in_=tri_ap), writes=["tri"])
        mk.dma("sync", lambda q: q.dma_start(out=msk[:], in_=msk_ap), writes=["msk"])
        mk.op("vector", lambda v: v.memset(sb["ones_col"][:], 1.0), writes=["ones_col"])
        mk.op("vector", lambda v: v.memset(sb["ones_row"][:], 1.0), writes=["ones_row"])

    R_X = A2[:, 0:2048]
    R_XLN = A2[:, 2048:4096]
    for ti in range(NT):
        st = lambda c, ti=ti: stats[:, ti * SC + c: ti * SC + c + 1]
        stk = lambda c, ti=ti: K("st", ti, c)
        xk = K("xt")
        mk.dma("sync", lambda q, ti=ti: q.dma_start(out=R_X, in_=x_in[ti * P:(ti + 1) * P, :]), writes=[xk])
        mk.op("vector", lambda v, st=st: v.reduce_sum(st(0), R_X, axis=AX.X), reads=[xk], writes=[stk(0)])
        mk.op("vector", lambda v, st=st: v.tensor_scalar(st(1), st(0), -1.0 / 2048.0, None, ALU.mult),
              reads=[stk(0)], writes=[stk(1)])
        mk.op("vector", lambda v, st=st: v.tensor_scalar(R_XLN, R_X, st(1), None, ALU.add),
              reads=[xk, stk(1)], writes=[K("xln")])
        mk.op("scalar", lambda s, st=st: s.activation(R_X, R_XLN, AF.Square, accum_out=st(7)),
              reads=[K("xln")], writes=[xk, stk(7)])
        mk.op("vector", lambda v, st=st: v.tensor_scalar(st(7), st(7), 1.0 / 2048.0, 1e-5, ALU.mult, ALU.add),
              reads=[stk(7)], writes=[stk(7)])
        mk.op("scalar", lambda s, st=st: s.sqrt(st(7), st(7)), reads=[stk(7)], writes=[stk(7)])
        mk.op("vector", lambda v, st=st: v.reciprocal(st(2), st(7)), reads=[stk(7)], writes=[stk(2)])
        mk.op("vector", lambda v, st=st: v.tensor_scalar(R_XLN, R_XLN, st(2), None, ALU.mult),
              reads=[K("xln"), stk(2)], writes=[K("xln")])
        for kc in range(NKC):
            b = 6 + kc % 2
            mk.op("tensor", lambda t, kc=kc, b=b: t.transpose(pb[b][:, 0:128], R_XLN[:, kc * 128:(kc + 1) * 128], idt[:]),
                  reads=[K("xln"), "iden"], writes=[K("pb", b)])
            col = (kc * NT + ti) * 128
            mk.op("vector", lambda v, col=col, b=b: v.tensor_copy(A1[:, col:col + 128], pb[b][:, 0:128]),
                  reads=[K("pb", b)], writes=[K("xlnT", kc, ti)])
    mk_barrier(mk)

    wbab = sb.get("wbab")
    if wbab is None:
        wbab = sb["wbab"] = mk.sbuf("wbab", [128, 512], BF16)
    mk.dma("gpsimd", lambda q: q.dma_start(out=wbab[:].rearrange("p (kc n) -> p kc n", kc=16),
                                           in_=wba.rearrange("(kc p) d -> p kc d", p=128)),
           writes=[K("wbab")])
    for ti in range(NT):
        for kc in range(NKC):
            col = (kc * NT + ti) * 128
            mk.op("tensor", lambda t, col=col, kc=kc: t.matmul(
                pb[0][:, 0:32], A1[:, col:col + 128], wbab[:, kc * 32:(kc + 1) * 32],
                start=(kc == 0), stop=(kc == NKC - 1)),
                reads=[K("xlnT", kc, ti), K("wbab")], writes=[K("pb", 0)])
        mk.op("vector", lambda v, ti=ti: v.tensor_copy(sb["ba"][:, ti * 32:(ti + 1) * 32], pb[0][:, 0:32]),
              reads=[K("pb", 0)], writes=[K("ba", ti)])

    for ob in range(4):
        par = ob % 2
        wzv = A1[:, VBASE + par * 8192: VBASE + par * 8192 + 16 * 512]
        wk = K("wz", par)
        mk.dma("gpsimd", lambda q, ob=ob, wzv=wzv: q.dma_start(
            out=wzv.rearrange("p (kc n) -> p kc n", kc=16),
            in_=wqkvz.rearrange("(kc p) d -> p kc d", p=128)[:, :, 4096 + ob * 512: 4096 + (ob + 1) * 512]),
            writes=[wk])
        for tig in range(0, NT, 4):
            tis = list(range(tig, min(tig + 4, NT)))
            for kc in range(NKC):
                for j, ti in enumerate(tis):
                    col = (kc * NT + ti) * 128
                    mk.op("tensor", lambda t, j=j, col=col, wzv=wzv, kc=kc: t.matmul(
                        pb[j][:, 0:512], A1[:, col:col + 128], wzv[:, kc * 512:(kc + 1) * 512],
                        start=(kc == 0), stop=(kc == NKC - 1)),
                        reads=[K("xlnT", kc, ti), wk], writes=[K("pb", j)])
            for j, ti in enumerate(tis):
                zs = sb["goutb"][:, j * 512:(j + 1) * 512]
                mk.op("vector", lambda v, j=j, zs=zs: v.tensor_copy(zs, pb[j][:, 0:512]),
                      reads=[K("pb", j)], writes=[K("zst", j)])
                mk.dma("scalar", lambda q, ti=ti, ob=ob, zs=zs: q.dma_start(
                    out=z_hbm[ti * P:(ti + 1) * P, ob * 512:(ob + 1) * 512], in_=zs),
                    reads=[K("zst", j)], writes=[K("zh", ti, ob)])
    mk_barrier(mk)

    NTB = NT * 128 // 512
    mk.dma("sync", lambda q: q.dma_start(out=sb["cw"][:], in_=convw_r), writes=[K("cw")])
    mk.dma("sync", lambda q: q.dma_start(out=sb["cb"][:], in_=convb_r), writes=[K("cb")])
    mk.op("vector", lambda v: v.memset(sb["cstg"][:, 0:3], 0.0), writes=[K("cpad")])
    for oc in range(32):
        ws = sb["wst"][:, 0:2048]
        wk = K("wqkv",)
        wcol = oc * 128 if oc < 16 else 2048 + (oc - 16) * 128
        mk.dma("gpsimd", lambda q, ws=ws, wcol=wcol: q.dma_start(
            out=ws.rearrange("p (kc n) -> p kc n", kc=16),
            in_=wqkvz.rearrange("(kc p) d -> p kc d", p=128)[:, :, wcol:wcol + 128]),
            writes=[wk])
        for kc in range(NKC):
            for tb in range(NTB):
                mk.op("tensor", lambda t, tb=tb, kc=kc, ws=ws: t.matmul(
                    pb[tb][:, 0:512], ws[:, kc * 128:(kc + 1) * 128],
                    A1[:, kc * NT * 128 + tb * 512: kc * NT * 128 + tb * 512 + 512],
                    start=(kc == 0), stop=(kc == NKC - 1)),
                    reads=[wk] + [K("xlnT", kc, t4) for t4 in range(tb * 4, tb * 4 + 4)],
                    writes=[K("pb", tb)])
        for tb in range(NTB):
            mk.op("vector", lambda v, tb=tb: v.tensor_copy(sb["cstg"][:, 3 + tb * 512: 3 + (tb + 1) * 512], pb[tb][:, 0:512]),
                  reads=[K("pb", tb)], writes=[K("cstg")])
        mk.op("vector", lambda v, oc=oc: v.tensor_scalar(sb["cacc"][:, 0:T], sb["cstg"][:, 0:T], sb["cw"][:, oc * 4:oc * 4 + 1], None, ALU.mult),
              reads=[K("cstg"), K("cpad"), K("cw")], writes=[K("cacc")])
        for j in range(1, 4):
            mk.op("vector", lambda v, j=j, oc=oc: v.scalar_tensor_tensor(
                sb["cacc"][:, 0:T], sb["cstg"][:, j:j + T], sb["cw"][:, oc * 4 + j:oc * 4 + j + 1], sb["cacc"][:, 0:T], ALU.mult, ALU.add),
                reads=[K("cstg"), K("cw"), K("cacc")], writes=[K("cacc")])
        mk.op("vector", lambda v, oc=oc: v.tensor_scalar(sb["cacc"][:, 0:T], sb["cacc"][:, 0:T], sb["cb"][:, oc:oc + 1], None, ALU.add),
              reads=[K("cacc"), K("cb")], writes=[K("cacc")])
        if oc >= 16:
            dst = A1[:, VBASE + (oc - 16) * NT * 128: VBASE + (oc - 15) * NT * 128]
            mk.op("scalar", lambda s, dst=dst: s.activation(dst, sb["cacc"][:, 0:T], AF.Sigmoid),
                  reads=[K("cacc")], writes=[K("vslot", oc - 16)])
            mk.op("vector", lambda v, dst=dst: v.tensor_mul(dst, dst, sb["cacc"][:, 0:T]),
                  reads=[K("cacc"), K("vslot", oc - 16)], writes=[K("vslot", oc - 16)])
        else:
            mk.op("scalar", lambda s: s.activation(sb["csil"][:, 0:T], sb["cacc"][:, 0:T], AF.Sigmoid),
                  reads=[K("cacc")], writes=[K("csil")])
            mk.op("vector", lambda v: v.tensor_mul(sb["csil"][:, 0:T], sb["csil"][:, 0:T], sb["cacc"][:, 0:T]),
                  reads=[K("cacc"), K("csil")], writes=[K("csil")])
            mk.dma("scalar", lambda q, oc=oc: q.dma_start(out=qk_hbm[oc * 128:(oc + 1) * 128, :], in_=sb["csil"][:, 0:T]),
                   reads=[K("csil")], writes=[K("qkh", oc)])
    mk_barrier(mk)

    for oc in range(16):
        mk.dma("sync", lambda q, oc=oc: q.dma_start(out=A1[:, oc * NT * 128:(oc + 1) * NT * 128],
                                                    in_=qk_hbm[oc * 128:(oc + 1) * 128, :]),
               reads=[K("qkh", oc)], writes=[K("qk", oc)])

    r16 = sb["r16"]
    mk.dma("sync", lambda q: q.dma_start(out=sb["rr"][0:1, 0:16], in_=dtb), writes=[K("dtbr")])
    mk.dma("sync", lambda q: q.dma_start(out=sb["rr"][0:1, 16:32], in_=alog), writes=[K("alr")])
    mk.op("scalar", lambda s: s.activation(sb["rr"][0:1, 16:32], sb["rr"][0:1, 16:32], AF.Exp),
          reads=[K("alr")], writes=[K("alr")])
    mk.op("vector", lambda v: v.tensor_scalar(sb["rr"][0:1, 16:32], sb["rr"][0:1, 16:32], -1.0, None, ALU.mult),
          reads=[K("alr")], writes=[K("alr")])
    mk.op("tensor", lambda t: t.matmul(pb[0][:, 0:32], sb["ones_row"][0:1, :], sb["rr"][0:1, 0:32], start=True, stop=True),
          reads=[K("dtbr"), K("alr"), "ones_row"], writes=[K("pb", 0)])
    mk.op("vector", lambda v: v.tensor_copy(r16[:, 0:32], pb[0][:, 0:32]),
          reads=[K("pb", 0)], writes=[K("r16c")])
    cm1 = mk.const(-1.0)
    c1p = mk.const(1.0)
    for ti in range(NT):
        bsl = sb["bb"][:, ti * 16:(ti + 1) * 16]
        gsl = sb["gb"][:, ti * 16:(ti + 1) * 16]
        mk.op("scalar", lambda s, ti=ti, bsl=bsl: s.activation(bsl, sb["ba"][:, ti * 32: ti * 32 + 16], AF.Sigmoid),
              reads=[K("ba", ti)], writes=[K("bbt", ti)])
        mk.op("vector", lambda v, ti=ti: v.tensor_add(r16[:, 32:48], sb["ba"][:, ti * 32 + 16: ti * 32 + 32], r16[:, 0:16]),
              reads=[K("ba", ti), K("r16c")], writes=[K("aa")])
        mk.op("vector", lambda v: v.tensor_scalar(r16[:, 48:64], r16[:, 32:48], 0.0, None, ALU.max),
              reads=[K("aa")], writes=[K("mx")])
        mk.op("vector", lambda v: v.scalar_tensor_tensor(r16[:, 32:48], r16[:, 32:48], -1.0, r16[:, 32:48], ALU.mult, ALU.max),
              reads=[K("aa")], writes=[K("aa")])
        mk.op("scalar", lambda s, cm1=cm1: s.activation(r16[:, 32:48], r16[:, 32:48], AF.Exp, scale=cm1),
              reads=[K("aa")] + mk.cdep(-1.0), writes=[K("aa")])
        mk.op("scalar", lambda s, c1p=c1p: s.activation(r16[:, 32:48], r16[:, 32:48], AF.Ln, bias=c1p),
              reads=[K("aa")] + mk.cdep(1.0), writes=[K("aa")])
        mk.op("vector", lambda v: v.tensor_add(r16[:, 32:48], r16[:, 32:48], r16[:, 48:64]),
              reads=[K("aa"), K("mx")], writes=[K("aa")])
        mk.op("vector", lambda v, gsl=gsl: v.tensor_mul(gsl, r16[:, 32:48], r16[:, 16:32]),
              reads=[K("aa"), K("r16c")], writes=[K("gbt", ti)])
    for ti in range(NT):
        gsl = sb["gb"][:, ti * 16:(ti + 1) * 16]
        for which, lcol in [(0, 0), (1, 259)]:
            bank = 1 + which
            mk.op("tensor", lambda t, gsl=gsl, lcol=lcol, bank=bank: t.matmul(
                pb[bank][:, 0:16], tri[:, lcol:lcol + 128], gsl, start=True, stop=True),
                reads=[K("gbt", ti), "tri"], writes=[K("pb", bank)])
            dst = (sb["gcum"] if which == 0 else sb["eww"])[:, ti * 16:(ti + 1) * 16]
            mk.op("vector", lambda v, dst=dst, bank=bank: v.tensor_copy(dst, pb[bank][:, 0:16]),
                  reads=[K("pb", bank)], writes=[K("gcum", ti) if which == 0 else K("gw", ti)])
        mk.op("vector", lambda v, ti=ti: v.scalar_tensor_tensor(
            sb["eww"][:, ti * 16:(ti + 1) * 16], sb["gb"][:, ti * 16:(ti + 1) * 16], -1.0,
            sb["eww"][:, ti * 16:(ti + 1) * 16], ALU.mult, ALU.add),
            reads=[K("gbt", ti), K("gw", ti)], writes=[K("gw", ti)])
        for half in range(2):
            c = 2 * ti + half
            mk.op("tensor", lambda t, gsl=gsl, half=half: t.matmul(
                pb[3][0:1, 0:16], tri[:, 256 + half:257 + half], gsl, start=True, stop=True),
                reads=[K("gbt", ti), "tri"], writes=[K("pb", 3)])
            mk.op("vector", lambda v, c=c: v.tensor_copy(sb["glx"][0:1, c * 16:(c + 1) * 16], pb[3][0:1, 0:16]),
                  reads=[K("pb", 3)], writes=[K("glx")])
    for ti in range(NT):
        sl = lambda t_, ti=ti: t_[:, ti * 16:(ti + 1) * 16]
        mk.op("scalar", lambda s, sl=sl: s.activation(sl(sb["eaa"]), sl(sb["gb"]), AF.Exp),
              reads=[K("gbt", ti)], writes=[K("ea", ti)])
        mk.op("scalar", lambda s, sl=sl: s.activation(sl(sb["ebb"]), sl(sb["bb"]), AF.Sqrt),
              reads=[K("bbt", ti)], writes=[K("sqb", ti)])
        mk.op("scalar", lambda s, sl=sl: s.activation(sl(sb["eww"]), sl(sb["eww"]), AF.Exp),
              reads=[K("gw", ti)], writes=[K("ew", ti)])
        mk.op("vector", lambda v, sl=sl: v.tensor_mul(sl(sb["beb"]), sl(sb["ebb"]), sl(sb["eaa"])),
              reads=[K("sqb", ti), K("ea", ti)], writes=[K("sea", ti)])
    mk.op("scalar", lambda s: s.activation(sb["glx"][0:1, 0:NC * 16], sb["glx"][0:1, 0:NC * 16], AF.Exp),
          reads=[K("glx")], writes=[K("glx")])
    for hb in range((NC * 16 + 511) // 512):
        w = min(512, NC * 16 - hb * 512)
        mk.op("tensor", lambda t, hb=hb, w=w: t.matmul(pb[4][:, 0:w], sb["ones_row"][0:1, :],
                                                       sb["glx"][0:1, hb * 512: hb * 512 + w], start=True, stop=True),
              reads=[K("glx"), "ones_row"], writes=[K("pb", 4)])
        mk.op("vector", lambda v, hb=hb, w=w: v.tensor_copy(sb["ebl"][:, hb * 512: hb * 512 + w], pb[4][:, 0:w]),
              reads=[K("pb", 4)], writes=[K("ebl")])
    mk_barrier(mk)

    for kh in range(16):
        isq = kh < 8
        slot = A1[:, kh * NT * 128:(kh + 1) * NT * 128]
        sq = A2[:, 0:2048]
        mk.op("scalar", lambda s, slot=slot, sq=sq: s.activation(sq[:, 0:T], slot, AF.Square),
              reads=[K("qk", kh)], writes=[K("sq")])
        for tb in range(NTB):
            mk.op("tensor", lambda t, tb=tb, sq=sq: t.matmul(pb[tb][0:1, 0:512], sb["ones_col"][:, 0:1],
                                                             sq[:, tb * 512:(tb + 1) * 512], start=True, stop=True),
                  reads=[K("sq"), "ones_col"], writes=[K("pb", tb)])
            mk.op("vector", lambda v, tb=tb: v.tensor_copy(sb["rr"][0:1, tb * 512:(tb + 1) * 512], pb[tb][0:1, 0:512]),
                  reads=[K("pb", tb)], writes=[K("rr")])
        mk.op("vector", lambda v: v.tensor_scalar(sb["rr"][0:1, 0:T], sb["rr"][0:1, 0:T], 1e-6, None, ALU.add),
              reads=[K("rr")], writes=[K("rr")])
        mk.op("scalar", lambda s: s.sqrt(sb["rr"][0:1, 0:T], sb["rr"][0:1, 0:T]), reads=[K("rr")], writes=[K("rr")])
        mk.op("vector", lambda v: v.reciprocal(sb["rr"][0:1, 0:T], sb["rr"][0:1, 0:T]), reads=[K("rr")], writes=[K("rr")])
        if isq:
            mk.op("vector", lambda v: v.tensor_scalar(sb["rr"][0:1, 0:T], sb["rr"][0:1, 0:T], float(128.0 ** -0.5), None, ALU.mult),
                  reads=[K("rr")], writes=[K("rr")])
        for tb in range(NTB):
            mk.op("tensor", lambda t, tb=tb: t.matmul(pb[4 + tb % 2][:, 0:512], sb["ones_row"][0:1, :],
                                                      sb["rr"][0:1, tb * 512:(tb + 1) * 512], start=True, stop=True),
                  reads=[K("rr"), "ones_row"], writes=[K("pb", 4 + tb % 2)])
            mk.op("vector", lambda v, tb=tb: v.tensor_copy(sb["rrb"][:, tb * 512:(tb + 1) * 512], pb[4 + tb % 2][:, 0:512]),
                  reads=[K("pb", 4 + tb % 2)], writes=[K("rrb")])
        mk.op("vector", lambda v, slot=slot: v.tensor_mul(slot, slot, sb["rrb"][:, 0:T]),
              reads=[K("qk", kh), K("rrb")], writes=[K("qk", kh)])
    mk_barrier(mk)

    for kh in range(8):
        qslot = A1[:, kh * NT * 128:(kh + 1) * NT * 128]
        kslot = A1[:, (8 + kh) * NT * 128:(9 + kh) * NT * 128]
        for nm, slot, dstt, dk in [("k", kslot, sb["ktk"], K("ktk"))]:
            mk.op("vector", lambda v, slot=slot, nm=nm: v.tensor_copy(A2[:, 0:T], slot),
                  reads=[K("qk", (0 if nm == "q" else 8) + kh)], writes=[K("cpf")])
            for tb2 in range(NT):
                b = 6 + tb2 % 2
                mk.op("tensor", lambda t, tb2=tb2, b=b: t.transpose(pb[b][:, 0:128], A2[:, tb2 * 128:(tb2 + 1) * 128], idt[:]),
                      reads=[K("cpf"), "iden"], writes=[K("pb", b)])
                mk.op("vector", lambda v, tb2=tb2, b=b, dstt=dstt: v.tensor_copy(dstt[:, tb2 * 128:(tb2 + 1) * 128], pb[b][:, 0:128]),
                      reads=[K("pb", b)], writes=[dk])
        for vh in [2 * kh, 2 * kh + 1]:
            vslot = A1[:, VBASE + vh * NT * 128: VBASE + (vh + 1) * NT * 128]
            mk.op("vector", lambda v, vslot=vslot: v.tensor_copy(A2[:, 0:T], vslot),
                  reads=[K("vslot", vh)], writes=[K("cpf")])
            for tb2 in range(NT):
                b = 6 + tb2 % 2
                mk.op("tensor", lambda t, tb2=tb2, b=b: t.transpose(pb[b][:, 0:128], A2[:, tb2 * 128:(tb2 + 1) * 128], idt[:]),
                      reads=[K("cpf"), "iden"], writes=[K("pb", b)])
                mk.op("vector", lambda v, tb2=tb2, b=b: v.tensor_copy(sb["vtk"][:, tb2 * 128:(tb2 + 1) * 128], pb[b][:, 0:128]),
                      reads=[K("pb", b)], writes=[K("vtk")])
            for ti in range(NT):
                mk.op("vector", lambda v, ti=ti, vh=vh: v.tensor_scalar(
                    A2[:, 2048 + ti * 128: 2048 + (ti + 1) * 128], sb["ktk"][:, ti * 128:(ti + 1) * 128],
                    sb["ebb"][:, ti * 16 + vh: ti * 16 + vh + 1], None, ALU.mult),
                    reads=[K("ktk"), K("sqb", ti)], writes=[K("kbf")])
            knbT = sb.get("knbT")
            if knbT is None:
                knbT = sb["knbT"] = mk.sbuf("knbT", [128, NT * 128], BF16)
            for tb2 in range(NT):
                b = 6 + tb2 % 2
                mk.op("tensor", lambda t, tb2=tb2, b=b: t.transpose(pb[b][:, 0:128], A2[:, 2048 + tb2 * 128: 2048 + (tb2 + 1) * 128], idt[:]),
                      reads=[K("kbf"), "iden"], writes=[K("pb", b)])
                mk.op("vector", lambda v, tb2=tb2, b=b, knbT=knbT: v.tensor_copy(knbT[:, tb2 * 128:(tb2 + 1) * 128], pb[b][:, 0:128]),
                      reads=[K("pb", b)], writes=[K("knbT")])
            mk.op("vector", lambda v: v.memset(sb["Sst"][:], 0.0), writes=[K("S")])
            mk.op("vector", lambda v: v.memset(sb["Sbf"][:], 0.0), writes=[K("Sb")])
            dw, dw2, df = sb["dw"], sb["dw2"], sb["df"]
            for c in range(NC):
                ti = c // 2
                base = (c % 2) * 64
                c64 = c * 64
                ecol = lambda t_, off=0: t_[base:base + 64, ti * 16 + vh: ti * 16 + vh + 1]
                gsl_c = sb["gcum"][base:base + 64, ti * 16 + vh: ti * 16 + vh + 1]
                mk.op("vector", lambda v, gsl_c=gsl_c: v.tensor_scalar(df[0:64, 508:509], gsl_c, -1.0, None, ALU.mult),
                      reads=[K("gcum", ti)], writes=[K("gneg")])
                mk.op("gpsimd", lambda g, gsl_c=gsl_c: g.affine_select(
                    df[0:64, 256:320], gsl_c.broadcast_to([64, 64]), pattern=[[-1, 64]],
                    compare_op=ALU.is_equal, fill=0.0, base=0, channel_multiplier=1),
                    reads=[K("gcum", ti)], writes=[K("dgp")])
                mk.op("gpsimd", lambda g: g.affine_select(
                    df[0:64, 320:384], df[0:64, 508:509].broadcast_to([64, 64]), pattern=[[-1, 64]],
                    compare_op=ALU.is_equal, fill=0.0, base=0, channel_multiplier=1),
                    reads=[K("gneg")], writes=[K("dgn")])
                mk.op("tensor", lambda t: t.matmul(pb[2][0:64, 0:64], df[0:64, 256:320], msk[0:64, 128:192], start=True, stop=False),
                      reads=[K("dgp"), "msk"], writes=[K("pb", 2)])
                mk.op("tensor", lambda t: t.matmul(pb[2][0:64, 0:64], msk[0:64, 128:192], df[0:64, 320:384], start=False, stop=True),
                      reads=[K("dgn"), "msk"], writes=[K("pb", 2)])
                mk.op("vector", lambda v: v.tensor_scalar(df[0:64, 64:128], pb[2][0:64, 0:64], 0.0, None, ALU.min),
                      reads=[K("pb", 2)], writes=[K("Gcm")])
                mk.op("vector", lambda v: v.tensor_scalar(df[0:64, 128:192], pb[2][0:64, 0:64], 0.0, None, ALU.max),
                      reads=[K("pb", 2)], writes=[K("Gcp")])
                mk.op("scalar", lambda s: s.activation(df[0:64, 384:448], df[0:64, 64:128], AF.Exp),
                      reads=[K("Gcm")], writes=[K("Dm")])
                mk.op("scalar", lambda s, cm1=cm1: s.activation(df[0:64, 448:512], df[0:64, 128:192], AF.Exp, scale=cm1[0:64, 0:1]),
                      reads=[K("Gcp")] + mk.cdep(-1.0), writes=[K("DmT")])
                mk.op("tensor", lambda t, c64=c64, knbT=knbT: t.matmul(
                    pb[0][0:64, 0:64], knbT[:, c64:c64 + 64], knbT[:, c64:c64 + 64], start=True, stop=True),
                    reads=[K("knbT")], writes=[K("pb", 0)])
                mk.op("vector", lambda v: v.tensor_mul(df[0:64, 0:64], pb[0][0:64, 0:64], df[0:64, 384:448]),
                      reads=[K("pb", 0), K("Dm")], writes=[K("t1")])
                mk.op("vector", lambda v: v.tensor_mul(dw[0:64, 0:64], df[0:64, 0:64], msk[0:64, 0:64]),
                      reads=[K("t1"), "msk"], writes=[K("X", 0)])
                mk.op("vector", lambda v: v.tensor_mul(df[0:64, 0:64], pb[0][0:64, 0:64], df[0:64, 448:512]),
                      reads=[K("pb", 0), K("DmT"), K("X", 0)], writes=[K("t1")])
                mk.op("vector", lambda v: v.tensor_mul(dw[0:64, 64:128], df[0:64, 0:64], msk[0:64, 192:256]),
                      reads=[K("t1"), "msk"], writes=[K("B", 0)])
                mk.op("tensor", lambda t, c64=c64, kslot=kslot, qslot=qslot: t.matmul(
                    pb[1][0:64, 0:64], kslot[:, c64:c64 + 64], qslot[:, c64:c64 + 64], start=True, stop=True),
                    reads=[K("qk", kh), K("qk", 8 + kh)], writes=[K("pb", 1)])
                mk.op("vector", lambda v: v.tensor_mul(df[0:64, 0:64], pb[1][0:64, 0:64], df[0:64, 448:512]),
                      reads=[K("pb", 1), K("DmT"), K("B", 0)], writes=[K("t1")])
                mk.op("vector", lambda v: v.tensor_mul(dw[0:64, 128:192], df[0:64, 0:64], msk[0:64, 64:128]),
                      reads=[K("t1"), "msk"], writes=[K("M2")])
                if sb.get("dbg2") is not None and vh == 0 and c < 2:
                    d2, d2b = sb["dbg2"], sb["dbg2b"]
                    co, cbo = c * 2048, c * 1024
                    mk.dma("gpsimd", lambda q, cbo=cbo, d2b=d2b: q.dma_start(out=d2b[0:64, cbo:cbo + 64], in_=dw[0:64, 0:64]), reads=[K("X", 0)], writes=["dbg2w"])
                    mk.dma("gpsimd", lambda q, cbo=cbo, d2b=d2b: q.dma_start(out=d2b[0:64, cbo + 64:cbo + 128], in_=dw[0:64, 64:128]), reads=[K("B", 0)], writes=["dbg2w"])
                    mk.dma("gpsimd", lambda q, cbo=cbo, d2b=d2b: q.dma_start(out=d2b[0:64, cbo + 128:cbo + 192], in_=dw[0:64, 128:192]), reads=[K("M2")], writes=["dbg2w"])
                    mk.dma("gpsimd", lambda q, co=co, d2=d2: q.dma_start(out=d2[0:64, co + 512:co + 576], in_=df[0:64, 384:448]), reads=[K("Dm")], writes=["dbg2w"])
                    mk.dma("gpsimd", lambda q, co=co, d2=d2: q.dma_start(out=d2[0:64, co + 576:co + 640], in_=df[0:64, 448:512]), reads=[K("DmT")], writes=["dbg2w"])
                mk.op("vector", lambda v, base=base, ti=ti, vh=vh: v.tensor_scalar(
                    dw2[0:64, 0:128], sb["vtk"][base:base + 64, ti * 128:(ti + 1) * 128],
                    sb["ebb"][base:base + 64, ti * 16 + vh: ti * 16 + vh + 1], None, ALU.mult),
                    reads=[K("vtk"), K("sqb", ti)], writes=[K("U")])
                mk.op("vector", lambda v, base=base, ti=ti, vh=vh: v.tensor_scalar(
                    dw2[0:64, 128:256], sb["ktk"][base:base + 64, ti * 128:(ti + 1) * 128],
                    sb["beb"][base:base + 64, ti * 16 + vh: ti * 16 + vh + 1], None, ALU.mult),
                    reads=[K("ktk"), K("sea", ti)], writes=[K("U")])
                xoff, boff = 0, 64
                for lev in range(6):
                    mk.op("tensor", lambda t, boff=boff: t.matmul(pb[3][0:64, 0:256], dw[0:64, boff:boff + 64],
                                                                  dw2[0:64, 0:256], start=True, stop=False),
                          reads=[K("B", 0), K("U")], writes=[K("pb", 3)])
                    mk.op("tensor", lambda t: t.matmul(pb[3][0:64, 0:256], idb[0:64, 0:64], dw2[0:64, 0:256],
                                                       start=False, stop=True),
                          reads=["idb", K("U")], writes=[K("pb", 3)])
                    mk.op("vector", lambda v: v.tensor_copy(dw2[0:64, 0:256], pb[3][0:64, 0:256]),
                          reads=[K("pb", 3)], writes=[K("U")])
                    if lev < 5:
                        nxo, nbo = (192, 256) if xoff == 0 else (0, 64)
                        mk.op("tensor", lambda t, xoff=xoff, boff=boff, nxo=nxo: t.matmul(
                            pb[4][0:64, 0:64], dw[0:64, boff:boff + 64], dw[0:64, xoff:xoff + 64], start=True, stop=True),
                            reads=[K("X", 0), K("B", 0)], writes=[K("pb", 4)])
                        mk.op("tensor", lambda t, xoff=xoff, boff=boff: t.matmul(
                            pb[5][0:64, 0:64], dw[0:64, xoff:xoff + 64], dw[0:64, boff:boff + 64], start=True, stop=True),
                            reads=[K("X", 0), K("B", 0)], writes=[K("pb", 5)])
                        mk.op("vector", lambda v, nxo=nxo: v.tensor_copy(dw[0:64, nxo:nxo + 64], pb[4][0:64, 0:64]),
                              reads=[K("pb", 4)], writes=[K("X", 0)])
                        mk.op("vector", lambda v, nbo=nbo: v.tensor_copy(dw[0:64, nbo:nbo + 64], pb[5][0:64, 0:64]),
                              reads=[K("pb", 5)], writes=[K("B", 0)])
                        xoff, boff = nxo, nbo
                mk.op("vector", lambda v, base=base, ti=ti, vh=vh: v.tensor_scalar(
                    df[0:64, 0:256], dw2[0:64, 0:256],
                    sb["ebb"][base:base + 64, ti * 16 + vh: ti * 16 + vh + 1], None, ALU.mult),
                    reads=[K("U"), K("sqb", ti)], writes=[K("W")])
                mk.op("tensor", lambda t: t.transpose(pb[6][:, 0:64], df[0:64, 128:256], idt[0:64, 0:64]),
                      reads=[K("W"), "iden"], writes=[K("pb", 6)])
                mk.op("vector", lambda v: v.tensor_copy(dw[:, 320:384], pb[6][:, 0:64]),
                      reads=[K("pb", 6)], writes=[K("kcdT")])
                mk.op("tensor", lambda t: t.matmul(pb[6][0:64, 0:128], dw[:, 320:384], sb["Sbf"][:], start=True, stop=True),
                      reads=[K("kcdT"), K("Sb")], writes=[K("pb", 6)])
                mk.op("vector", lambda v: v.scalar_tensor_tensor(dw2[0:64, 256:384], pb[6][0:64, 0:128], -1.0, df[0:64, 0:128], ALU.mult, ALU.add),
                      reads=[K("pb", 6), K("W")], writes=[K("vnew")])
                mk.op("tensor", lambda t, c64=c64, qslot=qslot: t.matmul(pb[7][0:64, 0:128], qslot[:, c64:c64 + 64], sb["Sbf"][:],
                                                            start=True, stop=True),
                      reads=[K("qk", kh), K("Sb")], writes=[K("pb", 7)])
                mk.op("vector", lambda v, base=base, ti=ti, vh=vh: v.tensor_scalar(
                    df[0:64, 256:384], pb[7][0:64, 0:128],
                    sb["eaa"][base:base + 64, ti * 16 + vh: ti * 16 + vh + 1], None, ALU.mult),
                    reads=[K("pb", 7), K("ea", ti)], writes=[K("o1"), K("dgp"), K("dgn")])
                mk.op("tensor", lambda t: t.matmul(pb[7][0:64, 0:128], dw[0:64, 128:192], dw2[0:64, 256:384],
                                                   start=True, stop=True),
                      reads=[K("M2"), K("vnew"), K("o1")], writes=[K("pb", 7)])
                oblk = vslot[base:base + 64, (c // 2) * 128:(c // 2) * 128 + 128]
                mk.op("vector", lambda v, oblk=oblk: v.tensor_add(oblk, pb[7][0:64, 0:128], df[0:64, 256:384]),
                      reads=[K("pb", 7), K("o1"), K("dgp"), K("dgn")], writes=[K("oc", vh, c)])
                mk.op("vector", lambda v, base=base, ti=ti, vh=vh: v.tensor_scalar(
                    dw[0:64, 448:576], sb["ktk"][base:base + 64, ti * 128:(ti + 1) * 128],
                    sb["eww"][base:base + 64, ti * 16 + vh: ti * 16 + vh + 1], None, ALU.mult),
                    reads=[K("ktk"), K("ew", ti)], writes=[K("kw")])
                mk.op("tensor", lambda t: t.matmul(pb[6][:, 0:128], dw[0:64, 448:576], dw2[0:64, 256:384],
                                                   start=True, stop=True),
                      reads=[K("kw"), K("vnew")], writes=[K("pb", 6)])
                mk.op("vector", lambda v, c=c, vh=vh: v.scalar_tensor_tensor(
                    sb["Sst"][:], sb["Sst"][:], sb["ebl"][:, c * 16 + vh: c * 16 + vh + 1], pb[6][:, 0:128], ALU.mult, ALU.add),
                      reads=[K("S"), K("ebl"), K("pb", 6)], writes=[K("S")])
                mk.op("vector", lambda v: v.tensor_copy(sb["Sbf"][:], sb["Sst"][:]),
                      reads=[K("S")], writes=[K("Sb")])
                if sb.get("dbg2") is not None and vh == 0 and c < 2:
                    d2, d2b = sb["dbg2"], sb["dbg2b"]
                    co, cbo, c64_ = c * 2048, c * 1024, c * 64
                    mk.dma("gpsimd", lambda q, cbo=cbo, d2b=d2b: q.dma_start(out=d2b[0:64, cbo + 192:cbo + 448], in_=dw2[0:64, 0:256]), reads=[K("U")], writes=["dbg2w"])
                    mk.dma("gpsimd", lambda q, cbo=cbo, d2b=d2b: q.dma_start(out=d2b[0:64, cbo + 448:cbo + 576], in_=dw2[0:64, 256:384]), reads=[K("vnew")], writes=["dbg2w"])
                    mk.dma("gpsimd", lambda q, cbo=cbo, d2b=d2b, c64_=c64_, qslot=qslot: q.dma_start(out=d2b[:, cbo + 576:cbo + 640], in_=qslot[:, c64_:c64_ + 64]), reads=[K("qk", kh)], writes=["dbg2w"])
                    mk.dma("gpsimd", lambda q, cbo=cbo, d2b=d2b, c64_=c64_, kslot=kslot: q.dma_start(out=d2b[:, cbo + 640:cbo + 704], in_=kslot[:, c64_:c64_ + 64]), reads=[K("qk", 8 + kh)], writes=["dbg2w"])
                    mk.dma("gpsimd", lambda q, cbo=cbo, d2b=d2b: q.dma_start(out=d2b[:, cbo + 704:cbo + 832], in_=sb["ktk"][:, 0:128]), reads=[K("ktk")], writes=["dbg2w"])
                    mk.dma("gpsimd", lambda q, cbo=cbo, d2b=d2b, c64_=c64_, knbT=knbT: q.dma_start(out=d2b[:, cbo + 832:cbo + 896], in_=knbT[:, c64_:c64_ + 64]), reads=[K("knbT")], writes=["dbg2w"])
                    mk.dma("gpsimd", lambda q, co=co, d2=d2: q.dma_start(out=d2[0:64, co:co + 256], in_=df[0:64, 0:256]), reads=[K("W")], writes=["dbg2w"])
                    mk.dma("gpsimd", lambda q, co=co, d2=d2: q.dma_start(out=d2[0:64, co + 256:co + 384], in_=df[0:64, 256:384]), reads=[K("o1")], writes=["dbg2w"])
                    mk.dma("gpsimd", lambda q, co=co, d2=d2: q.dma_start(out=d2[:, co + 384:co + 512], in_=sb["Sst"][:]), reads=[K("S")], writes=["dbg2w"])
                    if c == 0:
                        mk.dma("gpsimd", lambda q, d2=d2: q.dma_start(out=d2[:, 1280:1280 + NT * 16], in_=sb["eaa"][:, 0:NT * 16]), reads=[K("ea", t_) for t_ in range(NT)], writes=["dbg2w"])
                        mk.dma("gpsimd", lambda q, d2=d2: q.dma_start(out=d2[:, 1344:1344 + NT * 16], in_=sb["ebb"][:, 0:NT * 16]), reads=[K("sqb", t_) for t_ in range(NT)], writes=["dbg2w"])
                        mk.dma("gpsimd", lambda q, d2=d2: q.dma_start(out=d2[:, 1408:1408 + NT * 16], in_=sb["beb"][:, 0:NT * 16]), reads=[K("sea", t_) for t_ in range(NT)], writes=["dbg2w"])
                        mk.dma("gpsimd", lambda q, d2=d2: q.dma_start(out=d2[:, 1472:1472 + NT * 16], in_=sb["eww"][:, 0:NT * 16]), reads=[K("ew", t_) for t_ in range(NT)], writes=["dbg2w"])
                        mk.dma("gpsimd", lambda q, d2=d2: q.dma_start(out=d2[:, 1536:1536 + 2 * NT * 16], in_=sb["ebl"][:, 0:2 * NT * 16]), reads=[K("ebl")], writes=["dbg2w"])
    mk_barrier(mk)

    if sb.get("dbg0") is not None:
        mk.dma("gpsimd", lambda q: q.dma_start(out=sb["dbg0"][:, 0:NT * 16],
            in_=sb["gb"][:, 0:NT * 16]), reads=[K("gbt", ti2) for ti2 in range(NT)], writes=["dbg_g"])
        mk.dma("gpsimd", lambda q: q.dma_start(out=sb["dbg0"][:, NT * 16:2 * NT * 16],
            in_=sb["bb"][:, 0:NT * 16]), reads=[K("bbt", ti2) for ti2 in range(NT)], writes=["dbg_b"])
        mk.dma("gpsimd", lambda q: q.dma_start(out=sb["dbg0"][:, 2 * NT * 16:3 * NT * 16],
            in_=sb["gcum"][:, 0:NT * 16]), reads=[K("gcum", ti2) for ti2 in range(NT)], writes=["dbg_gc"])
        mk.dma("gpsimd", lambda q: q.dma_start(out=sb["dbg1"][:, 0:16 * NT * 128],
            in_=A1[:, VBASE:VBASE + 16 * NT * 128]),
            reads=[K("oc", vh2, c2) for vh2 in range(16) for c2 in range(NC)], writes=["dbg_out"])
    WOBASE = 0 if NT >= 16 else 2 * VBASE
    for kc in range(16):
        mk.dma("gpsimd", lambda q, kc=kc: q.dma_start(
            out=A1[:, WOBASE + kc * 2048:WOBASE + (kc + 1) * 2048],
            in_=wout[kc * 128:(kc + 1) * 128, :]),
            writes=[K("wo", kc)])
    mk.dma("sync", lambda q: q.dma_start(out=sb["rr"][0:1, 0:128], in_=nw_ap), writes=[K("rr")])
    mk.op("tensor", lambda t: t.matmul(pb[0][:, 0:128], sb["ones_row"][0:1, :], sb["rr"][0:1, 0:128], start=True, stop=True),
          reads=[K("rr"), "ones_row"], writes=[K("pb", 0)])
    mk.op("vector", lambda v: v.tensor_copy(sb["nwb"][:], pb[0][:, 0:128]),
          reads=[K("pb", 0)], writes=[K("nwb")])
    for ti in range(NT):
        mk.dma("scalar", lambda q, ti=ti: q.dma_start(out=sb["zt"][:], in_=z_hbm[ti * P:(ti + 1) * P, :]),
               reads=[K("zh", ti, 0), K("zh", ti, 1), K("zh", ti, 2), K("zh", ti, 3)], writes=[K("ztk")])
        mk.op("scalar", lambda s: s.activation(sb["goutb"][:], sb["zt"][:], AF.Sigmoid),
              reads=[K("ztk")], writes=[K("zsil")])
        mk.op("vector", lambda v: v.tensor_mul(sb["goutb"][:], sb["goutb"][:], sb["zt"][:]),
              reads=[K("ztk"), K("zsil")], writes=[K("zsil")])
        for vh in range(16):
            blk = A1[:, VBASE + vh * NT * 128 + ti * 128: VBASE + vh * NT * 128 + (ti + 1) * 128]
            vc = stats[:, ti * SC + 8: ti * SC + 9]
            mk.op("scalar", lambda s, blk=blk, vc=vc: s.activation(A2[:, 4224:4352], blk, AF.Square, accum_out=vc),
                  reads=[K("oc", vh, 2 * ti), K("oc", vh, 2 * ti + 1)], writes=[K("sqd"), K("vc")])
            mk.op("vector", lambda v, vc=vc: v.tensor_scalar(vc, vc, 1.0 / 128.0, 1e-6, ALU.mult, ALU.add),
                  reads=[K("vc")], writes=[K("vc")])
            mk.op("scalar", lambda s, vc=vc: s.sqrt(vc, vc), reads=[K("vc")], writes=[K("vc")])
            mk.op("vector", lambda v, vc=vc: v.reciprocal(vc, vc), reads=[K("vc")], writes=[K("vc")])
            gsl = A2[:, vh * 128:(vh + 1) * 128]
            mk.op("vector", lambda v, blk=blk, vc=vc, gsl=gsl: v.tensor_scalar(gsl, blk, vc, None, ALU.mult),
                  reads=[K("oc", vh, 2 * ti), K("oc", vh, 2 * ti + 1), K("vc")], writes=[K("gg")])
            mk.op("vector", lambda v, gsl=gsl: v.tensor_mul(gsl, gsl, sb["nwb"][:]),
                  reads=[K("gg"), K("nwb")], writes=[K("gg")])
            mk.op("vector", lambda v, gsl=gsl, vh=vh: v.tensor_mul(gsl, gsl, sb["goutb"][:, vh * 128:(vh + 1) * 128]),
                  reads=[K("gg"), K("zsil")], writes=[K("gated", vh)])
        for vh in range(16):
            b = 6 + vh % 2
            gsl = A2[:, vh * 128:(vh + 1) * 128]
            mk.op("tensor", lambda t, gsl=gsl, b=b: t.transpose(pb[b][:, 0:128], gsl, idt[:]),
                  reads=[K("gated", vh), "iden"], writes=[K("pb", b)])
            mk.op("vector", lambda v, vh=vh, b=b: v.tensor_copy(sb["gT"][:, vh * 128:(vh + 1) * 128], pb[b][:, 0:128]),
                  reads=[K("pb", b)], writes=[K("gT")])
        for ob in range(4):
            for kc in range(16):
                mk.op("tensor", lambda t, ob=ob, kc=kc: t.matmul(
                    pb[ob][:, 0:512], sb["gT"][:, kc * 128:(kc + 1) * 128],
                    A1[:, WOBASE + kc * 2048 + ob * 512: WOBASE + kc * 2048 + (ob + 1) * 512],
                    start=(kc == 0), stop=(kc == 15)),
                    reads=[K("gT"), K("wo", kc)], writes=[K("pb", ob)])
            mk.op("vector", lambda v, ob=ob: v.tensor_copy(sb["evo"][:, 0:512], pb[ob][:, 0:512]),
                  reads=[K("pb", ob)], writes=[K("evo",)])
            mk.dma("scalar", lambda q, ti=ti, ob=ob: q.dma_start(
                out=pc_out[ti * P:(ti + 1) * P, ob * 512:(ob + 1) * 512],
                in_=sb["evo"][:, 0:512]),
                reads=[K("evo",)], writes=[K("pc", ti, ob)])
    return [K("pc", ti, ob) for ti in range(NT) for ob in range(4)]



def _host_consts():
    iden = np.eye(128, dtype=np.float32)
    idb = np.eye(64, dtype=ml_dtypes.bfloat16)
    tri = np.zeros((128, 387), np.float32)
    u64 = np.triu(np.ones((64, 64), np.float32))
    tri[0:64, 0:64] = u64; tri[64:128, 64:128] = u64
    tri[63, 256] = 1.0; tri[127, 257] = 1.0
    tri[63, 259:323] = 1.0; tri[127, 323:387] = 1.0
    msk = np.zeros((64, 256), np.float32)
    msk[0:64, 0:64] = -np.tril(np.ones((64, 64)), -1)
    msk[0:64, 192:256] = -np.triu(np.ones((64, 64)), 1)
    msk[0:64, 64:128] = np.triu(np.ones((64, 64)), 0)
    msk[0:64, 128:192] = 1.0
    return iden, idb, tri, msk


def _build_ffn0(n_extras):
    nc = bass.Bass("TRN2", target_bir_lowering=False, debug=False)
    d = lambda n, s, dt=F32, kind="ExternalInput": nc.dram_tensor(n, list(s), dt, kind=kind).ap()
    x_in = d("x", [1024, 2048])
    extras = [d(f"e{i}", [1024, 2048]) for i in range(n_extras)]
    wgT = d("wgT", [2048, 8192]); wuT = d("wuT", [2048, 8192]); wdT = d("wdT", [8192, 2048])
    idap = d("iden", [128, 128])
    x_out = d("xout", [1024, 2048], F32, "ExternalOutput")
    gu = d("gu_scratch", [1024, 8192], F32, "ExternalOutput")
    with ExitStack() as ctx:
        mk = MK(nc, ctx)
        sb = alloc_ffn0_sb(mk, 8)
        outs = emit_ffn0(mk, "f", 1024, x_in, extras, x_out, wgT, wuT, wdT, gu, idap, sb)
        mk.wait_all_outputs(outs)
        mk.finalize()
    return nc


def _build_att0():
    nc = bass.Bass("TRN2", target_bir_lowering=False, debug=False)
    d = lambda n, s, dt=F32, kind="ExternalInput": nc.dram_tensor(n, list(s), dt, kind=kind).ap()
    T = 2048; NT = 16
    x_in = d("x", [T, 2048])
    wq_d = d("wqkvz", [2048, 6144]); wba_d = d("wba", [2048, 32])
    cw_d = d("convw", [128, 128]); cb_d = d("convb", [128, 32])
    al_d = d("alog", [1, 16]); dt_d = d("dtb", [1, 16]); nw_d = d("nw", [1, 128])
    wo_d = d("wout", [2048, 2048])
    id_d = d("iden", [128, 128]); idb_d = d("idb", [64, 64], BF16)
    tri_d = d("tri", [128, 387]); msk_d = d("msk", [64, 256])
    pc_d = d("pc", [T, 2048], F32, "ExternalOutput")
    zh_d = d("z_hbm", [T, 2048], BF16, "ExternalOutput")
    qkh_d = d("qk_hbm", [2048, T], BF16, "ExternalOutput")
    with ExitStack() as ctx:
        mk = MK(nc, ctx)
        sb = alloc_att0_sb(mk, NT)
        outs = emit_att0(mk, "a0", NT, x_in, wq_d, wba_d, cw_d, cb_d, al_d, dt_d, nw_d,
                         wo_d, zh_d, qkh_d, pc_d, id_d, idb_d, tri_d, msk_d, sb)
        mk.wait_all_outputs(outs)
        mk.finalize()
    return nc


def _build_att1():
    nc = bass.Bass("TRN2", target_bir_lowering=False, debug=False)
    d = lambda n, s, dt=F32, kind="ExternalInput": nc.dram_tensor(n, list(s), dt, kind=kind).ap()
    T = 2048; NT = 16
    x_in = d("x", [T, 2048])
    waT_d = d("waT", [2048, 1280]); woT_d = d("woT", [1024, 2048])
    id_d = d("iden", [128, 128])
    pc = d("pc", [T, 2048], F32, "ExternalOutput")
    with ExitStack() as ctx:
        mk = MK(nc, ctx)
        sb = alloc_att1_sb(mk, NT)
        outs = emit_att1(mk, "a1", NT, x_in, waT_d, woT_d, pc, id_d, sb)
        mk.wait_all_outputs(outs)
        mk.finalize()
    return nc


def _build_thought(agg, n_parts):
    nc = bass.Bass("TRN2", target_bir_lowering=False, debug=False)
    d = lambda n, s, dt=F32, kind="ExternalInput": nc.dram_tensor(n, list(s), dt, kind=kind).ap()
    xin = d("xin", [4, 2048])
    parts = [d(f"p{i}", [4, 2048]) for i in range(n_parts)]
    d1s = d("d1s", [2048, 1024]); d1b = d("d1b", [1, 1024]); d2s = d("d2s", [1024, 2048])
    png = d("png", [1, 2048]); pnb = d("pnb", [1, 2048]); idap = d("iden", [128, 128])
    aggw = d("aggw", [2048, 2048]) if agg else None
    aggb = d("aggb", [1, 2048]) if agg else None
    ing = d("ing", [1, 2048]) if agg else None
    inb = d("inb", [1, 2048]) if agg else None
    d2b = None if agg else d("d2b", [1, 2048])
    outp = d("outp", [4, 2048], F32, "ExternalOutput")
    outt = d("outt", [4, 2048], F32, "ExternalOutput")
    with ExitStack() as ctx:
        mk = MK(nc, ctx)
        sb = alloc_thought_sb(mk)
        outs = emit_thought_iter(mk, "th", agg, xin, parts, d1s, d1b, d2s, png, pnb,
                                 outp, outt, agg_w_ap=aggw, agg_b_ap=aggb,
                                 ing_ap=ing, inb_ap=inb, d2b_ap=d2b, iden_ap=idap, sb=sb)
        mk.wait_all_outputs(outs)
        mk.finalize()
    return nc


def _build_ffn1z(n_extras):
    nc = bass.Bass("TRN2", target_bir_lowering=False, debug=False)
    d = lambda n, s, dt=F32, kind="ExternalInput": nc.dram_tensor(n, list(s), dt, kind=kind).ap()
    x = d("x", [1024, 2048])
    extras = [d(f"e{i}", [1024, 2048]) for i in range(n_extras)]
    th = d("th", [4, 2048])
    parts = [d(f"p{i}", [4, 2048]) for i in range(8)]
    d2b = d("d2b", [1, 2048]); bcw = d("bcw", [2048, 2048]); bcb = d("bcb", [1, 2048])
    gw = d("gw", [2048, 2048]); gwb = d("gwb", [1, 2048])
    og = d("og", [1, 2048]); obt = d("obt", [1, 2048])
    selb = d("selb", [4, 1]); idap = d("iden", [128, 128])
    xout = d("xout", [1024, 2048], F32, "ExternalOutput")
    with ExitStack() as ctx:
        mk = MK(nc, ctx)
        sb = alloc_ffn1z_sb(mk)
        outs = emit_ffn1_final(mk, "fz", 1024, selb, x, th, parts, d2b, bcw, bcb,
                               gw, gwb, og, obt, xout, idap, sb, x_extras=extras)
        mk.wait_all_outputs(outs)
        mk.finalize()
    return nc


_dbg_stages = {}


def _att0_maps(x1f, inputs, iden, idb, tri, msk):
    wq = np.asarray(inputs["L0_attn_w_qkvz"], np.float32)
    wba = np.asarray(inputs["L0_attn_w_ba"], np.float32)
    cw = np.asarray(inputs["L0_attn_conv_w"], np.float32)
    cb = np.asarray(inputs["L0_attn_conv_b"], np.float32)
    alog = np.asarray(inputs["L0_attn_A_log"], np.float32)
    dtb = np.asarray(inputs["L0_attn_dt_bias"], np.float32)
    nw = np.asarray(inputs["L0_attn_norm_weight"], np.float32)
    wo = np.asarray(inputs["L0_attn_w_out"], np.float32)
    maps = []
    for c in range(8):
        b, h = c // 2, c % 2
        qs = slice(h * 1024, (h + 1) * 1024)
        ks = slice(2048 + h * 1024, 2048 + (h + 1) * 1024)
        vs = slice(4096 + h * 2048, 4096 + (h + 1) * 2048)
        zs = slice(8192 + h * 2048, 8192 + (h + 1) * 2048)
        wqkvz_s = np.ascontiguousarray(np.concatenate([wq[:, qs], wq[:, ks], wq[:, vs], wq[:, zs]], axis=1))
        wba_s = np.ascontiguousarray(np.concatenate(
            [wba[:, h * 16:(h + 1) * 16], wba[:, 32 + h * 16:32 + (h + 1) * 16]], axis=1))
        cw_s = np.concatenate([cw[:, qs], cw[:, ks], cw[:, vs]], -1)
        cb_s = np.concatenate([cb[qs], cb[ks], cb[vs]])
        convw_r = np.ascontiguousarray(cw_s.T.reshape(32, 128, 4).transpose(1, 0, 2).reshape(128, 128))
        convb_r = np.ascontiguousarray(cb_s.reshape(32, 128).T)
        maps.append({
            "x": np.ascontiguousarray(x1f[b * 2048:(b + 1) * 2048]),
            "wqkvz": wqkvz_s, "wba": wba_s, "convw": convw_r, "convb": convb_r,
            "alog": np.ascontiguousarray(alog[h * 16:(h + 1) * 16].reshape(1, 16)),
            "dtb": np.ascontiguousarray(dtb[h * 16:(h + 1) * 16].reshape(1, 16)),
            "nw": np.ascontiguousarray(nw.reshape(1, 128)),
            "wout": np.ascontiguousarray(wo[h * 2048:(h + 1) * 2048, :]),
            "iden": iden, "idb": idb, "tri": tri, "msk": msk})
    return maps


def _run_thought_chain(x_last, part2, inputs, iden):
    r1 = lambda a: np.ascontiguousarray(np.asarray(a, np.float32).reshape(1, -1))
    d1 = np.asarray(inputs["L1_ffn_d1_w"], np.float32)
    d1b = np.asarray(inputs["L1_ffn_d1_b"], np.float32)
    d2 = np.asarray(inputs["L1_ffn_d2_w"], np.float32)
    base = dict(png=r1(inputs["L1_ffn_pn_gamma"]), pnb=r1(inputs["L1_ffn_pn_beta"]), iden=iden)
    slc = []
    for c in range(8):
        m = dict(base)
        m["d1s"] = np.ascontiguousarray(d1[:, c * 1024:(c + 1) * 1024])
        m["d1b"] = r1(d1b[c * 1024:(c + 1) * 1024])
        m["d2s"] = np.ascontiguousarray(d2[c * 1024:(c + 1) * 1024, :])
        slc.append(m)
    nc_a = _build_thought(True, len(part2))
    maps = []
    for c in range(8):
        m = dict(slc[c])
        m.update(xin=x_last, aggw=np.ascontiguousarray(np.asarray(inputs["L1_ffn_agg_w"], np.float32)),
                 aggb=r1(inputs["L1_ffn_agg_b"]), ing=r1(inputs["L1_ffn_in_gamma"]),
                 inb=r1(inputs["L1_ffn_in_beta"]))
        for i, p in enumerate(part2):
            m[f"p{i}"] = p
        maps.append(m)
    res = run_bass_kernel_spmd(nc_a, maps, core_ids=list(range(8)))
    th = np.asarray(res.results[0]["outt"], np.float32)
    parts = [np.asarray(res.results[c]["outp"], np.float32) for c in range(8)]
    nc_i = _build_thought(False, 8)
    for _ in range(3):
        maps = []
        for c in range(8):
            m = dict(slc[c])
            m.update(xin=th, d2b=r1(inputs["L1_ffn_d2_b"]))
            for i in range(8):
                m[f"p{i}"] = parts[i]
            maps.append(m)
        res = run_bass_kernel_spmd(nc_i, maps, core_ids=list(range(8)))
        th = np.asarray(res.results[0]["outt"], np.float32)
        parts = [np.asarray(res.results[c]["outp"], np.float32) for c in range(8)]
    return th, parts


def _run_ffn1z(xf, x_extras_f, th, parts, inputs, iden):
    r1 = lambda a: np.ascontiguousarray(np.asarray(a, np.float32).reshape(1, -1))
    nc_z = _build_ffn1z(len(x_extras_f))
    maps = []
    for c in range(8):
        sel = np.zeros((4, 1), np.float32); sel[c // 2, 0] = 1.0
        m = dict(x=np.ascontiguousarray(xf[c * 1024:(c + 1) * 1024]), th=th,
                 d2b=r1(inputs["L1_ffn_d2_b"]),
                 bcw=np.ascontiguousarray(np.asarray(inputs["L1_ffn_bc_w"], np.float32)),
                 bcb=r1(inputs["L1_ffn_bc_b"]),
                 gw=np.ascontiguousarray(np.asarray(inputs["L1_ffn_gate_w"], np.float32)),
                 gwb=r1(inputs["L1_ffn_gate_b"]),
                 og=r1(inputs["L1_ffn_out_gamma"]), obt=r1(inputs["L1_ffn_out_beta"]),
                 selb=sel, iden=iden)
        for i in range(8):
            m[f"p{i}"] = parts[i]
        for i, ef in enumerate(x_extras_f):
            m[f"e{i}"] = np.ascontiguousarray(ef[c * 1024:(c + 1) * 1024])
        maps.append(m)
    res = run_bass_kernel_spmd(nc_z, maps, core_ids=list(range(8)))
    out = np.zeros((8192, 2048), np.float32)
    for c in range(8):
        out[c * 1024:(c + 1) * 1024] = np.asarray(res.results[c]["xout"], np.float32)
    return out


def kernel_multilaunch(**inputs):
    iden, idb, tri, msk = _host_consts()
    x0 = np.ascontiguousarray(np.asarray(inputs["L0_attn_hidden_states"], np.float32).reshape(8192, 2048))
    wgT = np.ascontiguousarray(np.asarray(inputs["L0_ffn_w_gate"], np.float32).T)
    wuT = np.ascontiguousarray(np.asarray(inputs["L0_ffn_w_up"], np.float32).T)
    wdT = np.ascontiguousarray(np.asarray(inputs["L0_ffn_w_down"], np.float32).T)

    nc_f0 = _build_ffn0(0)
    maps = [{"x": np.ascontiguousarray(x0[c * 1024:(c + 1) * 1024]),
             "wgT": wgT, "wuT": wuT, "wdT": wdT, "iden": iden} for c in range(8)]
    res = run_bass_kernel_spmd(nc_f0, maps, core_ids=list(range(8)))
    x1 = np.zeros((8192, 2048), np.float32)
    for c in range(8):
        x1[c * 1024:(c + 1) * 1024] = np.asarray(res.results[c]["xout"], np.float32)
    _dbg_stages["x1"] = x1

    nc_a0 = _build_att0()
    res = run_bass_kernel_spmd(nc_a0, _att0_maps(x1, inputs, iden, idb, tri, msk), core_ids=list(range(8)))
    pa = [[np.asarray(res.results[b * 2 + h]["pc"], np.float32) for h in range(2)] for b in range(4)]
    _dbg_stages["pa"] = pa

    nc_f2 = _build_ffn0(2)
    maps = []
    for c in range(8):
        b, hh = c // 2, c % 2
        rs = slice(hh * 1024, (hh + 1) * 1024)
        maps.append({"x": np.ascontiguousarray(x1[c * 1024:(c + 1) * 1024]),
                     "e0": np.ascontiguousarray(pa[b][0][rs]), "e1": np.ascontiguousarray(pa[b][1][rs]),
                     "wgT": wgT, "wuT": wuT, "wdT": wdT, "iden": iden})
    res = run_bass_kernel_spmd(nc_f2, maps, core_ids=list(range(8)))
    x3 = np.zeros((8192, 2048), np.float32)
    for c in range(8):
        x3[c * 1024:(c + 1) * 1024] = np.asarray(res.results[c]["xout"], np.float32)
    _dbg_stages["x3"] = x3

    x3_last = np.ascontiguousarray(x3[2047::2048])
    th, parts = _run_thought_chain(x3_last, [], inputs, iden)
    x4 = _run_ffn1z(x3, [], th, parts, inputs, iden)
    _dbg_stages["x4"] = x4

    wat = np.asarray(inputs["L1_attn_w_attn"], np.float32).T
    wo1t = np.asarray(inputs["L1_attn_w_out"], np.float32).T
    nc_a1 = _build_att1()
    maps = []
    for c in range(8):
        b, h = c // 2, c % 2
        maps.append({"x": np.ascontiguousarray(x4[b * 2048:(b + 1) * 2048]),
                     "waT": np.ascontiguousarray(np.concatenate(
                         [wat[:, h * 1024:(h + 1) * 1024], wat[:, 2048:2304]], axis=1)),
                     "woT": np.ascontiguousarray(wo1t[h * 1024:(h + 1) * 1024, :]),
                     "iden": iden})
    res = run_bass_kernel_spmd(nc_a1, maps, core_ids=list(range(8)))
    qa = [[np.asarray(res.results[b * 2 + h]["pc"], np.float32) for h in range(2)] for b in range(4)]
    _dbg_stages["qa"] = qa

    x4_last = np.ascontiguousarray(x4[2047::2048])
    q0_last = np.ascontiguousarray(np.concatenate([qa[b][0][2047:2048] for b in range(4)], axis=0))
    q1_last = np.ascontiguousarray(np.concatenate([qa[b][1][2047:2048] for b in range(4)], axis=0))
    th, parts = _run_thought_chain(x4_last, [q0_last, q1_last], inputs, iden)
    q0f = np.concatenate([qa[b][0] for b in range(4)], axis=0)
    q1f = np.concatenate([qa[b][1] for b in range(4)], axis=0)
    out = _run_ffn1z(x4, [q0f, q1f], th, parts, inputs, iden)
    return out.reshape(4, 2048, 2048)



def _emit_thought_chain(mk, tag, thr_ap, agg_parts, d1s_, d1b_, d2s_, png, pnb,
                        aggw, aggb, ing, inb, d2b, iden, rowsA, rowsB, tv, sbt):
    rows = [rowsA, rowsB]
    tcur = thr_ap
    for it in range(4):
        agg = (it == 0)
        wr = rows[it % 2]
        rd = rows[(it - 1) % 2]
        for s in range(8):
            if agg:
                emit_thought_iter(mk, f"{tag}i{it}s{s}", True, tcur, agg_parts,
                                  d1s_[s], d1b_[s], d2s_[s], png, pnb,
                                  wr[s], tv[it], agg_w_ap=aggw, agg_b_ap=aggb,
                                  ing_ap=ing, inb_ap=inb, d2b_ap=None,
                                  iden_ap=iden, sb=sbt)
            else:
                emit_thought_iter(mk, f"{tag}i{it}s{s}", False, tcur,
                                  [rd[ss] for ss in range(8)],
                                  d1s_[s], d1b_[s], d2s_[s], png, pnb,
                                  wr[s], tv[it], d2b_ap=d2b, iden_ap=iden, sb=sbt)
            mk_barrier(mk)
        tcur = tv[it]
    return tv[3], rows[3 % 2]


def _build_fused():
    nc = bass.Bass("TRN2", target_bir_lowering=False, debug=False)
    d = lambda n, s, dt=F32, kind="ExternalInput": nc.dram_tensor(n, list(s), dt, kind=kind).ap()
    x0 = d("x0", [2048, 2048])
    wgT = d("wgT", [2048, 8192]); wuT = d("wuT", [2048, 8192]); wdT = d("wdT", [8192, 2048])
    a0w = []
    for h in range(2):
        a0w.append(dict(
            wq=d(f"a0_wqkvz{h}", [2048, 6144]), wba=d(f"a0_wba{h}", [2048, 32]),
            cw=d(f"a0_convw{h}", [128, 128]), cb=d(f"a0_convb{h}", [128, 32]),
            al=d(f"a0_alog{h}", [1, 16]), dtb=d(f"a0_dtb{h}", [1, 16]),
            wo=d(f"a0_wout{h}", [2048, 2048])))
    nw = d("a0_nw", [1, 128])
    waT = [d(f"a1_waT{h}", [2048, 1280]) for h in range(2)]
    woT = [d(f"a1_woT{h}", [1024, 2048]) for h in range(2)]
    d1s_ = [d(f"d1s{s}", [2048, 1024]) for s in range(8)]
    d1b_ = [d(f"d1b{s}", [1, 1024]) for s in range(8)]
    d2s_ = [d(f"d2s{s}", [1024, 2048]) for s in range(8)]
    aggw = d("aggw", [2048, 2048]); aggb = d("aggb", [1, 2048])
    ing = d("ing", [1, 2048]); inb = d("inb", [1, 2048])
    png = d("png", [1, 2048]); pnb = d("pnb", [1, 2048]); d2b = d("d2b", [1, 2048])
    bcw = d("bcw", [2048, 2048]); bcb = d("bcb", [1, 2048])
    gww = d("gw", [2048, 2048]); gwb = d("gwb", [1, 2048])
    og = d("og", [1, 2048]); obt = d("obt", [1, 2048])
    selb = d("selb", [4, 1])
    iden = d("iden", [128, 128]); idb = d("idb", [64, 64], BF16)
    tri = d("tri", [128, 387]); msk = d("msk", [64, 256])
    I = lambda n, s, dt=F32: nc.dram_tensor(n, list(s), dt, kind="Internal").ap()
    x1 = I("x1", [2048, 2048]); p0 = I("p0", [2048, 2048]); p1 = I("p1", [2048, 2048])
    x3 = I("x3", [2048, 2048]); x4 = I("x4", [2048, 2048])
    q0 = I("q0", [2048, 2048]); q1 = I("q1", [2048, 2048])
    gu = I("gu", [1024, 8192]); zh = I("zhs", [2048, 2048], BF16); qkh = I("qkhs", [2048, 2048], BF16)
    thr = I("thr", [4, 2048]); x5r = I("x5r", [4, 2048])
    q0x4 = I("q0x4", [4, 2048]); q1x4 = I("q1x4", [4, 2048])
    rowsA = [I(f"rowA{s}", [4, 2048]) for s in range(8)]
    rowsB = [I(f"rowB{s}", [4, 2048]) for s in range(8)]
    tv1 = [I(f"tv1_{t}", [4, 2048]) for t in range(4)]
    tv2 = [I(f"tv2_{t}", [4, 2048]) for t in range(4)]
    out = d("out", [2048, 2048], F32, "ExternalOutput")

    with ExitStack() as ctx:
        mk = MK(nc, ctx)
        for v in (-1.0, 1.0, 0.5, -0.5):
            mk.const(v)
        with mk.arena("f1_"):
            sb = alloc_ffn0_sb(mk, 8)
            emit_ffn0(mk, "f1a", 1024, x0[0:1024, :], [], x1[0:1024, :], wgT, wuT, wdT, gu, iden, sb)
            mk_barrier(mk)
            emit_ffn0(mk, "f1b", 1024, x0[1024:2048, :], [], x1[1024:2048, :], wgT, wuT, wdT, gu, iden, sb)
        mk_barrier(mk)
        with mk.arena("a0_"):
            sba = alloc_att0_sb(mk, 16)
            for h, pc in ((0, p0), (1, p1)):
                w = a0w[h]
                emit_att0(mk, f"a0h{h}", 16, x1, w["wq"], w["wba"], w["cw"], w["cb"],
                          w["al"], w["dtb"], nw, w["wo"], zh, qkh, pc, iden, idb, tri, msk, sba)
                mk_barrier(mk)
        with mk.arena("f2_"):
            sb = alloc_ffn0_sb(mk, 8)
            emit_ffn0(mk, "f2a", 1024, x1[0:1024, :], [p0[0:1024, :], p1[0:1024, :]],
                      x3[0:1024, :], wgT, wuT, wdT, gu, iden, sb)
            mk_barrier(mk)
            emit_ffn0(mk, "f2b", 1024, x1[1024:2048, :], [p0[1024:2048, :], p1[1024:2048, :]],
                      x3[1024:2048, :], wgT, wuT, wdT, gu, iden, sb)
        mk_barrier(mk)
        for i in range(4):
            mk.dma("sync", lambda q, i=i: q.dma_start(out=thr[i:i + 1, :], in_=x3[2047:2048, :]),
                   writes=["thr"])
        mk_barrier(mk)
        with mk.arena("t1_"):
            sbt = alloc_thought_sb(mk)
            th3, parts3 = _emit_thought_chain(mk, "t1", thr, [], d1s_, d1b_, d2s_, png, pnb,
                                              aggw, aggb, ing, inb, d2b, iden, rowsA, rowsB, tv1, sbt)
        mk_barrier(mk)
        with mk.arena("z1_"):
            sbz = alloc_ffn1z_sb(mk)
            emit_ffn1_final(mk, "z1", 2048, selb, x3, th3, parts3, d2b, bcw, bcb,
                            gww, gwb, og, obt, x4, iden, sbz)
        mk_barrier(mk)
        with mk.arena("a1_"):
            sb1 = alloc_att1_sb(mk, 16)
            emit_att1(mk, "a1h0", 16, x4, waT[0], woT[0], q0, iden, sb1)
            mk_barrier(mk)
            emit_att1(mk, "a1h1", 16, x4, waT[1], woT[1], q1, iden, sb1)
        mk_barrier(mk)
        for i in range(4):
            mk.dma("sync", lambda q, i=i: q.dma_start(out=x5r[i:i + 1, :], in_=x4[2047:2048, :]), writes=["x5r"])
            mk.dma("sync", lambda q, i=i: q.dma_start(out=q0x4[i:i + 1, :], in_=q0[2047:2048, :]), writes=["q0x4"])
            mk.dma("sync", lambda q, i=i: q.dma_start(out=q1x4[i:i + 1, :], in_=q1[2047:2048, :]), writes=["q1x4"])
        mk_barrier(mk)
        with mk.arena("t2_"):
            sbt = alloc_thought_sb(mk)
            th3b, parts3b = _emit_thought_chain(mk, "t2", x5r, [q0x4, q1x4], d1s_, d1b_, d2s_,
                                                png, pnb, aggw, aggb, ing, inb, d2b, iden,
                                                rowsA, rowsB, tv2, sbt)
        mk_barrier(mk)
        with mk.arena("z2_"):
            sbz = alloc_ffn1z_sb(mk)
            outs = emit_ffn1_final(mk, "z2", 2048, selb, x4, th3b, parts3b, d2b, bcw, bcb,
                                   gww, gwb, og, obt, out, iden, sbz, x_extras=[q0, q1])
        mk.wait_all_outputs(outs)
        mk.finalize()
        nops = {e: len(mk.ops[e]) for e in mk.ops}
    return nc, nops


def _fused_maps(inputs, iden, idb, tri, msk):
    r1 = lambda a: np.ascontiguousarray(np.asarray(a, np.float32).reshape(1, -1))
    x0f = np.asarray(inputs["L0_attn_hidden_states"], np.float32)
    wgT = np.ascontiguousarray(np.asarray(inputs["L0_ffn_w_gate"], np.float32).T)
    wuT = np.ascontiguousarray(np.asarray(inputs["L0_ffn_w_up"], np.float32).T)
    wdT = np.ascontiguousarray(np.asarray(inputs["L0_ffn_w_down"], np.float32).T)
    wq = np.asarray(inputs["L0_attn_w_qkvz"], np.float32)
    wba = np.asarray(inputs["L0_attn_w_ba"], np.float32)
    cw = np.asarray(inputs["L0_attn_conv_w"], np.float32)
    cb = np.asarray(inputs["L0_attn_conv_b"], np.float32)
    alog = np.asarray(inputs["L0_attn_A_log"], np.float32)
    dtb = np.asarray(inputs["L0_attn_dt_bias"], np.float32)
    wo0 = np.asarray(inputs["L0_attn_w_out"], np.float32)
    wat = np.asarray(inputs["L1_attn_w_attn"], np.float32).T
    wo1t = np.asarray(inputs["L1_attn_w_out"], np.float32).T
    d1 = np.asarray(inputs["L1_ffn_d1_w"], np.float32)
    d1b = np.asarray(inputs["L1_ffn_d1_b"], np.float32)
    d2 = np.asarray(inputs["L1_ffn_d2_w"], np.float32)
    base = {"wgT": wgT, "wuT": wuT, "wdT": wdT, "a0_nw": r1(inputs["L0_attn_norm_weight"]),
            "iden": iden, "idb": idb, "tri": tri, "msk": msk,
            "aggw": np.ascontiguousarray(np.asarray(inputs["L1_ffn_agg_w"], np.float32)),
            "aggb": r1(inputs["L1_ffn_agg_b"]), "ing": r1(inputs["L1_ffn_in_gamma"]),
            "inb": r1(inputs["L1_ffn_in_beta"]), "png": r1(inputs["L1_ffn_pn_gamma"]),
            "pnb": r1(inputs["L1_ffn_pn_beta"]), "d2b": r1(inputs["L1_ffn_d2_b"]),
            "bcw": np.ascontiguousarray(np.asarray(inputs["L1_ffn_bc_w"], np.float32)),
            "bcb": r1(inputs["L1_ffn_bc_b"]),
            "gw": np.ascontiguousarray(np.asarray(inputs["L1_ffn_gate_w"], np.float32)),
            "gwb": r1(inputs["L1_ffn_gate_b"]), "og": r1(inputs["L1_ffn_out_gamma"]),
            "obt": r1(inputs["L1_ffn_out_beta"]),
            "selb": np.array([[1.0], [0.0], [0.0], [0.0]], np.float32)}
    for h in range(2):
        qs = slice(h * 1024, (h + 1) * 1024)
        ks = slice(2048 + h * 1024, 2048 + (h + 1) * 1024)
        vs = slice(4096 + h * 2048, 4096 + (h + 1) * 2048)
        zs = slice(8192 + h * 2048, 8192 + (h + 1) * 2048)
        base[f"a0_wqkvz{h}"] = np.ascontiguousarray(np.concatenate(
            [wq[:, qs], wq[:, ks], wq[:, vs], wq[:, zs]], axis=1))
        base[f"a0_wba{h}"] = np.ascontiguousarray(np.concatenate(
            [wba[:, h * 16:(h + 1) * 16], wba[:, 32 + h * 16:32 + (h + 1) * 16]], axis=1))
        cw_s = np.concatenate([cw[:, qs], cw[:, ks], cw[:, vs]], -1)
        cb_s = np.concatenate([cb[qs], cb[ks], cb[vs]])
        base[f"a0_convw{h}"] = np.ascontiguousarray(cw_s.T.reshape(32, 128, 4).transpose(1, 0, 2).reshape(128, 128))
        base[f"a0_convb{h}"] = np.ascontiguousarray(cb_s.reshape(32, 128).T)
        base[f"a0_alog{h}"] = np.ascontiguousarray(alog[h * 16:(h + 1) * 16].reshape(1, 16))
        base[f"a0_dtb{h}"] = np.ascontiguousarray(dtb[h * 16:(h + 1) * 16].reshape(1, 16))
        base[f"a0_wout{h}"] = np.ascontiguousarray(wo0[h * 2048:(h + 1) * 2048, :])
        base[f"a1_waT{h}"] = np.ascontiguousarray(np.concatenate(
            [wat[:, h * 1024:(h + 1) * 1024], wat[:, 2048:2304]], axis=1))
        base[f"a1_woT{h}"] = np.ascontiguousarray(wo1t[h * 1024:(h + 1) * 1024, :])
    for s in range(8):
        base[f"d1s{s}"] = np.ascontiguousarray(d1[:, s * 1024:(s + 1) * 1024])
        base[f"d1b{s}"] = r1(d1b[s * 1024:(s + 1) * 1024])
        base[f"d2s{s}"] = np.ascontiguousarray(d2[s * 1024:(s + 1) * 1024, :])
    maps = []
    for c in range(8):
        m = dict(base)
        m["x0"] = np.ascontiguousarray(x0f[c // 2])
        maps.append(m)
    return maps


def kernel(**inputs):
    iden, idb, tri, msk = _host_consts()
    nc, nops = _build_fused()
    maps = _fused_maps(inputs, iden, idb, tri, msk)
    res = run_bass_kernel_spmd(nc, maps, core_ids=list(range(8)))
    out = np.zeros((4, 2048, 2048), np.float32)
    for b in range(4):
        out[b] = np.asarray(res.results[2 * b]["out"], np.float32)
    return out
